# version 09
# baseline (speedup 1.0000x reference)
import math
from contextlib import ExitStack

import numpy as np
import concourse.bass as bass
import concourse.mybir as mybir
from concourse.bass import broadcast_tensor_aps
from concourse.bass_utils import run_bass_kernel_spmd

F32 = mybir.dt.float32
BF16 = mybir.dt.bfloat16
AF = mybir.ActivationFunctionType
ALU = mybir.AluOpType
AX = mybir.AxisListType

T, D = 4096, 1024
NH, DH = 16, 64
NCORE = 8
NT = T // 128
NTS = T // 512
NDC = D // 128
EPS = 1e-5
DEV = {}



class Bld:

    ENG = {"t": "tensor", "v": "vector", "s": "scalar", "g": "gpsimd", "y": "sync"}

    def __init__(self, nc, block):
        self.nc, self.block = nc, block
        self.phase = 0
        self.sems = {}
        self.cnt = {}
        self.lastw = {}
        self.grp_map = {}
        self.grp_next = 0
        self.phase_keys = set()

    def _sem(self, key):
        if key not in self.sems:
            self.sems[key] = self.nc.alloc_semaphore(
                name="sm_%s_%s_%s" % (key[0], key[1], key[2]))
            self.cnt[key] = 0
        return self.sems[key]

    def emit(self, eng, fn, waits=(), inc="c", grp=None):
        if inc == "d":
            if grp == "sh":
                gi = "sh"
            else:
                gi = self.grp_map.get(grp)
                if gi is None:
                    gi = self.grp_next
                    self.grp_map[grp] = gi
                    self.grp_next += 1
            key = (eng, "d", gi)
        else:
            key = (eng, "c", self.phase % 3)
        if inc:
            sem = self._sem(key)
            step = 16 if inc == "d" else 1
            self.phase_keys.add(key)
        out = []

        def body(e):
            for tk in waits:
                if tk is None:
                    continue
                k, v = tk
                if v <= 0:
                    continue
                if self.lastw.get((eng, k), 0) < v:
                    e.wait_ge(self.sems[k], v)
                    self.lastw[(eng, k)] = v
            if inc == "d":
                fn(_DmaWrap(e, self, sem, key))
                out.append((key, self.cnt[key]))
            else:
                inst = fn(e)
                if inc:
                    self.cnt[key] += step
                    inst.then_inc(sem, step)
                    out.append((key, self.cnt[key]))

        getattr(self.block, self.ENG[eng])(body)
        return out[0] if out else None

    def dma(self, eng, out, in_, waits=(), grp=None):
        return self.emit(eng, lambda e: e.dma_start(out=out, in_=in_), waits,
                         inc="d", grp=grp)

    def tok_all(self, eng, kind="c", grp=None):
        if kind == "d":
            key = (eng, "d", self.grp_map.get(grp, -1))
        else:
            key = (eng, "c", self.phase % 3)
        return (key, self.cnt.get(key, 0))

    def barrier(self):
        finals = [(k, self.cnt[k]) for k in sorted(self.phase_keys, key=str)
                  if self.cnt.get(k, 0) > 0]
        for eng in ("t", "v", "s", "g", "y"):
            def body(e, fins=tuple(finals), eng=eng):
                for k, v in fins:
                    if self.lastw.get((eng, k), 0) < v:
                        e.wait_ge(self.sems[k], v)
                        self.lastw[(eng, k)] = v
            getattr(self.block, self.ENG[eng])(body)
        self.phase += 1
        self.phase_keys = set()
        self.grp_map = {}
        self.grp_next = 0


class _DmaWrap:

    def __init__(self, e, bld, sem, key):
        self._e, self._bld, self._sem, self._key = e, bld, sem, key

    def dma_start(self, **kw):
        inst = self._e.dma_start(**kw)
        self._bld.cnt[self._key] += 16
        inst.then_inc(self._sem, 16)
        return inst

    def __getattr__(self, n):
        return getattr(self._e, n)


class Slot:
    __slots__ = ("prod", "cons", "ap")

    def __init__(self):
        self.prod = None
        self.cons = []
        self.ap = None


def slots(n):
    return [Slot() for _ in range(n)]


class Ctx:
    def __init__(self, b):
        self.b = b

    def produce(self, dst, eng, fn, waits=(), inc="c", first=True, last=True):
        b = self.b
        w = list(waits)
        if first:
            w += dst.cons + [dst.prod]
        tok = b.emit(eng, fn, waits=w, inc=inc if last else None, grp=id(dst))
        if last:
            dst.prod = tok
            dst.cons = []
        return tok

    def consume(self, srcs, eng, fn, waits=(), inc="c"):
        if isinstance(srcs, Slot):
            srcs = [srcs]
        w = [s.prod for s in srcs] + list(waits)
        tok = self.b.emit(eng, fn, waits=w, inc=inc, grp=id(srcs[0]))
        if tok:
            for s in srcs:
                s.cons.append(tok)
        return tok

    def xfer(self, srcs, dst, eng, fn, waits=(), inc="c", first=True, last=True):
        if isinstance(srcs, Slot):
            srcs = [srcs]
        b = self.b
        w = [s.prod for s in srcs] + list(waits)
        if first:
            w += dst.cons + [dst.prod]
        tok = b.emit(eng, fn, waits=w, inc=inc, grp=id(dst))
        if tok:
            for s in srcs:
                s.cons.append(tok)
        if last:
            dst.prod = tok
            dst.cons = []
        return tok


class LNKit:

    def __init__(self, nc, es, tag):
        self.trash = es.enter_context(nc.sbuf_tensor(f"ln_tr_{tag}", [128, D], F32))
        self.colbuf = es.enter_context(nc.sbuf_tensor(f"ln_c_{tag}", [128, 24], F32))
        self.cslots = slots(2)
        self.idx = 0

    def run(self, c, xs, out, outs, extra_waits=()):
        b = c.b
        i = self.idx % 2
        self.idx += 1
        cs = self.cslots[i]
        col = self.colbuf[:, i * 12:i * 12 + 12]
        (ss, sm, mean, em2, m2, var, rv, rstd,
         nmr, negm) = (col[:, j:j + 1] for j in range(10))
        x_ap = xs.ap

        t1a = c.xfer([xs], cs, "v",
                     lambda e: e.tensor_mul(self.trash[:], x_ap, x_ap),
                     waits=extra_waits, inc="c")
        def fsums(e):
            e.reduce_sum(ss, self.trash[:], axis=AX.X)
            return e.reduce_sum(sm, x_ap, axis=AX.X)
        t2 = c.xfer([xs], cs, "v", fsums, waits=(t1a,), inc="c", first=False)

        def f2(e):
            e.tensor_scalar_mul(mean, sm, 1.0 / D)
            e.tensor_scalar_mul(em2, ss, 1.0 / D)
            e.tensor_mul(m2, mean, mean)
            e.tensor_sub(var, em2, m2)
            e.tensor_scalar_add(var, var, EPS)
            return e.tensor_scalar_mul(negm, mean, -1.0)
        t3 = c.produce(cs, "g", f2, waits=(t2,), inc="c", first=False)
        t3b = c.produce(cs, "v", lambda e: e.reciprocal(rv, var),
                        waits=(t3,), first=False)
        t4 = c.produce(cs, "s",
                       lambda e: e.activation(rstd, rv, AF.Sqrt),
                       waits=(t3b,), first=False)
        t5 = c.produce(cs, "g", lambda e: e.tensor_mul(nmr, negm, rstd),
                       waits=(t4,), first=False)
        t6 = c.xfer([xs, cs], outs, "v",
                    lambda e: e.tensor_scalar_mul(out, x_ap, rstd),
                    waits=(t4,), inc="c")
        oo, nn = broadcast_tensor_aps(out, nmr)
        t7 = c.xfer([cs], outs, "v",
                    lambda e: e.tensor_add(oo, oo, nn),
                    waits=(t5, t6), first=False, last=True)
        return t7


class SB:

    def __init__(self, nc, es, name, cols, nslots=1):
        self.h = es.enter_context(nc.sbuf_tensor(name, [128, cols], F32))
        self.sl = slots(nslots)

    def __getitem__(self, idx):
        return self.h[idx]


def _attach(slot, ap):
    slot.ap = ap
    return slot



def build(dump=None):
    nc = bass.Bass("TRN2", target_bir_lowering=False)

    dt = {}
    def din(name, shape):
        dt[name] = nc.dram_tensor(name, shape, F32, kind="ExternalInput").ap()
    def dscr(name, shape):
        dt[name] = nc.dram_tensor(name, shape, F32, kind="Internal").ap()

    din("x_in", [T, D])
    din("wqkv", [D, 3 * D])
    din("wout", [D, D])
    din("r_w", [D, 64])
    din("r_b", [1, 64])
    din("w1", [64, D, 256])
    din("b1T", [128, 128])
    din("w2", [64, 256, D])
    din("b2", [64, D])
    din("sw1", [2, D, 2048])
    din("sb1T", [128, 32])
    din("sw2", [2, 2048, D])
    din("sb2r", [1, 2048])
    din("wq1T", [D, D])
    din("wk1T", [D, D])
    din("wv1T", [D, D])
    din("wo1T", [D, D])
    din("ncst", [1, 16])
    din("w1f", [D, 4096])
    din("b1fT", [128, 32])
    din("w2f", [4096, D])
    din("b2f", [1, D])
    din("eye", [128, 128])

    dscr("x_s", [T, D])
    dt["xh_s"] = nc.dram_tensor("xh_s", [D, T], BF16, kind="Internal").ap()
    dt["xl_s"] = nc.dram_tensor("xl_s", [D, T], BF16, kind="Internal").ap()
    dscr("qT_s", [D, T])
    dscr("kT_s", [D, T])
    dscr("v_s", [T, D])
    dscr("OT_s", [D, T])
    dscr("xnT_s", [D, T])
    dscr("acc_s", [T, D])
    dscr("cw_s", [64, T])
    dscr("q2T_s", [D, T])
    dscr("k2_s", [T, D])
    dscr("v2_s", [T, D])

    y = nc.dram_tensor("y", [T, D], F32, kind="ExternalOutput").ap()
    dbg = None
    if dump is not None:
        shp = dict(x_s=[T, D], qT_s=[D, T], kT_s=[D, T], v_s=[T, D],
                   OT_s=[D, T], xnT_s=[D, T], acc_s=[T, D], cw_s=[64, T],
                   q2T_s=[D, T], k2_s=[T, D], v2_s=[T, D], y=[T, D])[dump[0]]
        dbg = nc.dram_tensor("dbg", shp, F32, kind="ExternalOutput").ap()

    gstack = ExitStack()
    done = [False]

    with nc.Block() as block:
        b = Bld(nc, block)
        c = Ctx(b)

        ones = gstack.enter_context(nc.sbuf_tensor("ones", [1, 128], F32))
        cpool = gstack.enter_context(nc.sbuf_tensor("cpool", [128, 2], F32))
        eye_sb = gstack.enter_context(nc.sbuf_tensor("eye_sb", [128, 128], F32))
        bias_sb = gstack.enter_context(nc.sbuf_tensor("bias_sb", [128, 16], F32))
        esMX = ExitStack()
        maxf = esMX.enter_context(nc.sbuf_tensor("maxf", [128, 128], F32))

        b.emit("v", lambda e: e.memset(ones[:], 1.0), inc=None)
        b.emit("v", lambda e: e.memset(cpool[:, 0:1], EPS), inc=None)
        b.emit("v", lambda e: e.memset(cpool[:, 1:2], 1.0 / 32.0), inc=None)
        b.eps_ap = cpool[:, 0:1]
        b.s32_ap = cpool[:, 1:2]
        b.emit("v", lambda e: e.memset(maxf[:], 0.0), inc="c")
        b.dma("y", eye_sb[:], dt["eye"][:, :])
        b.barrier()

        def finish(ph):
            if done[0] or dump is None or dump[1] != ph:
                return False
            src = dt[dump[0]] if dump[0] != "y" else y
            R, C = src.shape
            with nc.sbuf_tensor("dbg_buf", [128, C], F32) as bb:
                prev = None
                for r in range(0, R, 128):
                    n = min(128, R - r)
                    tl = b.dma("y", bb[0:n, :], src[r:r + n, :],
                               waits=(prev,))
                    prev = b.dma("g", dbg[r:r + n, :], bb[0:n, :], waits=(tl,))
                b.barrier()
            done[0] = True
            return True

        esA = ExitStack()
        wq_sb = esA.enter_context(nc.sbuf_tensor("wq_sb", [128, 8 * 3072], F32))
        x_sl = esA.enter_context(nc.sbuf_tensor("x_sl", [128, 4 * 1024], F32))
        xn_sl = esA.enter_context(nc.sbuf_tensor("xn_sl", [128, 4 * 1024], F32))
        xnT = esA.enter_context(nc.sbuf_tensor("xnT", [128, 4096], F32))
        ev = esA.enter_context(nc.sbuf_tensor("ev", [128, 4 * 512], F32))
        vT_b = esA.enter_context(nc.sbuf_tensor("vT_b", [128, 2 * 512], F32))
        v_sl = esA.enter_context(nc.sbuf_tensor("v_sl", [128, 4096], F32))
        lnA = LNKit(nc, esA, "A")
        ps_tr = [esA.enter_context(nc.psum_tensor(f"psA_tr{i}", [128, 512], F32))
                 for i in range(2)]
        ps_p = [esA.enter_context(nc.psum_tensor(f"psA_p{i}", [128, 512], F32))
                for i in range(4)]

        s_w = Slot()
        s_x = [_attach(Slot(), x_sl[:, tt * 1024:(tt + 1) * 1024]) for tt in range(4)]
        s_xn = [_attach(Slot(), xn_sl[:, tt * 1024:(tt + 1) * 1024]) for tt in range(4)]
        s_xnT = [Slot() for _ in range(NDC)]
        s_pstr = slots(2)
        s_psp = slots(4)
        s_ev = slots(4)
        s_vT = slots(2)
        s_vblk = [slots(8) for _ in range(4)]

        def wload(e):
            for dc in range(NDC):
                inst = e.dma_start(out=wq_sb[:, dc * 3072:(dc + 1) * 3072],
                                   in_=dt["wqkv"][dc * 128:(dc + 1) * 128, :])
            return inst
        c.produce(s_w, "y", wload, inc="d")

        def loadx(ts):
            for tt in range(4):
                c.produce(s_x[tt], "y", lambda e, ts=ts, tt=tt: e.dma_start(
                    out=s_x[tt].ap,
                    in_=dt["x_in"][ts * 512 + tt * 128: ts * 512 + (tt + 1) * 128, :]),
                    inc="d")

        for ts in range(NTS):
            loadx(ts)
            for tt in range(4):
                lnA.run(c, s_x[tt], s_xn[tt].ap, s_xn[tt])
            for dc in range(NDC):
                pt = s_pstr[dc % 2]
                def ftr(e, dc=dc):
                    for tt in range(4):
                        inst = e.transpose(
                            ps_tr[dc % 2][:, tt * 128:(tt + 1) * 128],
                            xn_sl[:, tt * 1024 + dc * 128: tt * 1024 + (dc + 1) * 128],
                            eye_sb[:])
                    return inst
                c.xfer(s_xn, pt, "t", ftr)
                c.xfer([pt], s_xnT[dc], "v",
                       lambda e, dc=dc: e.tensor_copy(
                           xnT[:, dc * 512:(dc + 1) * 512], ps_tr[dc % 2][:]))
            for ec in range(24):
                pp = s_psp[ec % 4]
                def fmm(e, ec=ec):
                    for dc in range(NDC):
                        inst = e.matmul(
                            ps_p[ec % 4][:],
                            wq_sb[:, dc * 3072 + ec * 128: dc * 3072 + (ec + 1) * 128],
                            xnT[:, dc * 512:(dc + 1) * 512],
                            start=(dc == 0), stop=(dc == NDC - 1))
                    return inst
                c.xfer(s_xnT + [s_w], pp, "t", fmm)
                if ec < 16:
                    eb = ev[:, (ec % 4) * 512:(ec % 4 + 1) * 512]
                    c.xfer([pp], s_ev[ec % 4], "v",
                           lambda e, eb=eb, ec=ec: e.tensor_copy(eb, ps_p[ec % 4][:]))
                    mcl = (0 if ec < 8 else 64) + (ec % 8) * 8 + ts
                    c.consume([s_ev[ec % 4]], "v",
                              lambda e, eb=eb, mcl=mcl: e.reduce_max(
                                  maxf[:, mcl:mcl + 1], eb, axis=AX.X,
                                  apply_absolute_value=True))
                    dst = dt["qT_s"] if ec < 8 else dt["kT_s"]
                    col = ec % 8
                    c.consume([s_ev[ec % 4]], "s", lambda e, dst=dst, col=col, ts=ts, eb=eb:
                              e.dma_start(out=dst[col * 128:(col + 1) * 128,
                                                  ts * 512:(ts + 1) * 512], in_=eb),
                              inc="d")
                else:
                    ebi = (ec - 16)
                    vb = vT_b[:, (ec % 2) * 512:(ec % 2 + 1) * 512]
                    c.xfer([pp], s_vT[ec % 2], "v",
                           lambda e, vb=vb, ec=ec: e.tensor_copy(vb, ps_p[ec % 4][:]))
                    for tt in range(4):
                        pt = s_pstr[tt % 2]
                        c.xfer([s_vT[ec % 2]], pt, "t",
                               lambda e, vb=vb, tt=tt, ebi=ebi: e.transpose(
                                   ps_tr[tt % 2][:, (ebi % 4) * 128:(ebi % 4 + 1) * 128],
                                   vb[:, tt * 128:(tt + 1) * 128], eye_sb[:]))
                        c.xfer([pt], s_vblk[tt][ebi], "v",
                               lambda e, tt=tt, ebi=ebi: e.tensor_copy(
                                   v_sl[:, tt * 1024 + ebi * 128:
                                        tt * 1024 + (ebi + 1) * 128],
                                   ps_tr[tt % 2][:, (ebi % 4) * 128:(ebi % 4 + 1) * 128]))
                    if ebi == 7:
                        for tt in range(4):
                            c.consume(s_vblk[tt], "s",
                                      lambda e, tt=tt, ts=ts: e.dma_start(
                                          out=dt["v_s"][ts * 512 + tt * 128:
                                                        ts * 512 + (tt + 1) * 128, :],
                                          in_=v_sl[:, tt * 1024:(tt + 1) * 1024]),
                                      inc="d")
        b.barrier()
        esA.close()
        if finish(1):
            return nc, gstack, dt

        esP = ExitStack()
        qmf = esP.enter_context(nc.sbuf_tensor("qmf", [128, 16], F32))
        mrow = esP.enter_context(nc.sbuf_tensor("mrow", [8, 256], F32))
        mcol = esP.enter_context(nc.sbuf_tensor("mcol", [8, 8], F32))
        brow = esP.enter_context(nc.sbuf_tensor("brow", [1, 16], F32))
        psN0 = esP.enter_context(nc.psum_tensor("psN0", [128, 256], F32))
        s_ps = Slot()
        s_mrow = Slot()
        s_mcol = Slot()
        s_brow = Slot()
        s_bias = Slot()

        s_qmf = Slot()
        def fqm(e):
            for ec in range(16):
                inst = e.reduce_max(qmf[:, ec:ec + 1],
                                    maxf[:, ec * 8:(ec + 1) * 8], axis=AX.X)
            return inst
        c.produce(s_qmf, "v", fqm)
        def ftr2(e):
            e.transpose(psN0[0:8, 0:128], qmf[:, 0:8], eye_sb[:])
            return e.transpose(psN0[0:8, 128:256], qmf[:, 8:16], eye_sb[:])
        c.xfer([s_qmf], s_ps, "t", ftr2)
        c.xfer([s_ps], s_mrow, "v",
               lambda e: e.tensor_copy(mrow[0:8, 0:256], psN0[0:8, 0:256]))

        def fb(e):
            e.reduce_max(mcol[0:8, 0:1], mrow[0:8, 0:64], axis=AX.X)
            e.reduce_max(mcol[0:8, 1:2], mrow[0:8, 64:128], axis=AX.X)
            e.reduce_max(mcol[0:8, 2:3], mrow[0:8, 128:192], axis=AX.X)
            return e.reduce_max(mcol[0:8, 3:4], mrow[0:8, 192:256], axis=AX.X)
        tb1 = c.xfer([s_mrow], s_mcol, "v", fb)
        def fb2(e):
            e.tensor_mul(mcol[0:8, 4:5], mcol[0:8, 0:1], mcol[0:8, 2:3])
            e.tensor_mul(mcol[0:8, 5:6], mcol[0:8, 1:2], mcol[0:8, 3:4])
            return e.tensor_scalar_mul(mcol[0:8, 4:6], mcol[0:8, 4:6], -2.0)
        c.produce(s_mcol, "g", fb2, waits=(tb1,), first=False)
        def ftr3(e):
            e.transpose(psN0[0:1, 0:8], mcol[0:8, 4:5], eye_sb[0:8, 0:8])
            return e.transpose(psN0[0:1, 8:16], mcol[0:8, 5:6], eye_sb[0:8, 0:8])
        c.xfer([s_mcol], s_ps, "t", ftr3)
        c.xfer([s_ps], s_brow, "v",
               lambda e: e.tensor_copy(brow[0:1, 0:16], psN0[0:1, 0:16]))
        c.xfer([s_brow], s_ps, "t",
               lambda e: e.matmul(psN0[0:128, 0:16], ones[0:1, 0:128],
                                  brow[0:1, 0:16], start=True, stop=True))
        c.xfer([s_ps], s_bias, "v",
               lambda e: e.tensor_copy(bias_sb[:], psN0[0:128, 0:16]))
        b.barrier()
        esP.close()
        esMX.close()

        esB = ExitStack()
        kT_p = esB.enter_context(nc.sbuf_tensor("kT_p", [128, 2 * 4096], F32))
        v_p = esB.enter_context(nc.sbuf_tensor("v_p", [128, 2 * 4096], F32))
        qT_p = esB.enter_context(nc.sbuf_tensor("qT_p", [128, 2 * 256], F32))
        P_sb = esB.enter_context(nc.sbuf_tensor("P_sb", [128, 4 * 4096], F32))
        acc8 = esB.enter_context(nc.sbuf_tensor("acc8", [128, 4 * 10], F32))
        rinv = esB.enter_context(nc.sbuf_tensor("rinvB", [128, 8], F32))
        PT_s = esB.enter_context(nc.sbuf_tensor("PT_s", [128, 2 * 2 * 256], F32))
        Oraw = esB.enter_context(nc.sbuf_tensor("Oraw", [128, 256], F32))
        Otok = esB.enter_context(nc.sbuf_tensor("Otok", [128, 256], F32))
        OTst = esB.enter_context(nc.sbuf_tensor("OTst", [128, 256], F32))
        psS = [esB.enter_context(nc.psum_tensor(f"psS{i}", [128, 512], F32))
               for i in range(3)]
        psPT = [esB.enter_context(nc.psum_tensor(f"psPT{i}", [128, 512], F32))
                for i in range(2)]
        psPT2 = [esB.enter_context(nc.psum_tensor(f"psPT2{i}", [128, 512], F32))
                 for i in range(2)]
        psO = esB.enter_context(nc.psum_tensor("psO", [128, 256], F32))

        def psPTb(j, par):
            return (psPT if par == 0 else psPT2)[j]

        s_kT = slots(2)
        s_vp = slots(2)
        s_qT = slots(2)
        s_P = slots(4)
        s_acc = slots(4)
        s_rin = slots(8)
        s_PT = [slots(2) for _ in range(2)]
        s_psS = slots(3)
        s_psPT = [slots(2) for _ in range(2)]
        s_psO = Slot()
        s_Oraw = Slot()
        s_Otk = slots(2)
        s_OTq = slots(2)

        def load_pair(p):
            i = p % 2
            c.produce(s_kT[i], "y", lambda e, p=p, i=i: e.dma_start(
                out=kT_p[:, i * 4096:(i + 1) * 4096],
                in_=dt["kT_s"][p * 128:(p + 1) * 128, :]), inc="d")
            def fv(e, p=p, i=i):
                for kc in range(32):
                    inst = e.dma_start(
                        out=v_p[:, i * 4096 + kc * 128: i * 4096 + (kc + 1) * 128],
                        in_=dt["v_s"][kc * 128:(kc + 1) * 128, p * 128:(p + 1) * 128])
                return inst
            c.produce(s_vp[i], "y", fv, inc="d")

        NBP = DEV.get("B_p", 8)
        NBQ = DEV.get("B_qs", 16)
        exp_tok = {}
        pend_onorm = [None]

        def load_q(p, qs):
            iq = qs % 2
            c.produce(s_qT[iq], "y", lambda e, p=p, qs=qs, iq=iq: e.dma_start(
                out=qT_p[:, iq * 256:(iq + 1) * 256],
                in_=dt["qT_s"][p * 128:(p + 1) * 128, qs * 256:(qs + 1) * 256]),
                inc="d")

        load_pair(0)
        load_q(0, 0)
        for p in range(NBP):
            i = p % 2
            if p + 1 < NBP:
                load_pair(p + 1)
            for qs in range(NBQ):
                iq = qs % 2
                if qs + 1 < NBQ:
                    load_q(p, qs + 1)
                elif p + 1 < NBP:
                    load_q(p + 1, 0)
                if pend_onorm[0] is not None:
                    pend_onorm[0]()
                    pend_onorm[0] = None
                def emit_pt(kc):
                    par = kc % 2
                    for j in range(2):
                        def ftp(e, kc=kc, j=j, par=par):
                            for qt in range(2):
                                inst = e.transpose(
                                    psPTb(j, par)[:, qt * 128:(qt + 1) * 128],
                                    P_sb[:, (qt * 2 + j) * 4096 + kc * 128:
                                         (qt * 2 + j) * 4096 + (kc + 1) * 128],
                                    eye_sb[:])
                            return inst
                        w = list(s_psPT[j][par].cons) + [s_psPT[j][par].prod,
                             exp_tok[(0, j, kc // 4)], exp_tok[(1, j, kc // 4)]]
                        tk = b.emit("t", ftp, waits=w, inc="c")
                        s_P[0 * 2 + j].cons.append(tk)
                        s_P[1 * 2 + j].cons.append(tk)
                        s_psPT[j][par].prod = tk
                        s_psPT[j][par].cons = []
                        c.xfer([s_psPT[j][par]], s_PT[j][par], "v",
                               lambda e, j=j, par=par: e.tensor_copy(
                                   PT_s[:, (j * 2 + par) * 256:(j * 2 + par + 1) * 256],
                                   psPTb(j, par)[:, 0:256]))

                def emit_pv(kc):
                    par = kc % 2
                    for j in range(2):
                        c.xfer([s_PT[j][par], s_vp[i]], s_psO, "t",
                               lambda e, kc=kc, j=j, par=par, i=i: e.matmul(
                                   psO[j * 64:(j + 1) * 64, :],
                                   v_p[:, i * 4096 + kc * 128 + j * 64:
                                       i * 4096 + kc * 128 + (j + 1) * 64],
                                   PT_s[:, (j * 2 + par) * 256:(j * 2 + par + 1) * 256],
                                   start=(kc == 0), stop=(kc == 31),
                                   tile_position=(0, j * 64)),
                               first=(kc == 0 and j == 0),
                               last=(kc == 31 and j == 1))

                for qt in range(2):
                    for kc8 in range(8):
                        for j in range(2):
                            sp = s_P[qt * 2 + j]
                            sa = s_acc[qt * 2 + j]
                            ips = (2 * kc8 + j) % 3
                            c.xfer([s_qT[iq], s_kT[i]], s_psS[ips], "t",
                                   lambda e, ips=ips, j=j, qt=qt, kc8=kc8, i=i, iq=iq:
                                   e.matmul(psS[ips][:],
                                            qT_p[j * 64:(j + 1) * 64,
                                                 iq * 256 + qt * 128: iq * 256 + (qt + 1) * 128],
                                            kT_p[j * 64:(j + 1) * 64,
                                                 i * 4096 + kc8 * 512: i * 4096 + (kc8 + 1) * 512],
                                            start=True, stop=True,
                                            tile_position=(j * 64, 0)))
                            wx = (list(sa.cons) + [sa.prod]) if kc8 == 0 else ()
                            tk = c.xfer([s_psS[ips]], sp, "s",
                                        lambda e, ips=ips, qt=qt, j=j, kc8=kc8, p=p:
                                        e.activation(
                                            P_sb[:, (qt * 2 + j) * 4096 + kc8 * 512:
                                                 (qt * 2 + j) * 4096 + (kc8 + 1) * 512],
                                            psS[ips][:], AF.Exp, scale=b.s32_ap,
                                            bias=bias_sb[:, j * 8 + p:j * 8 + p + 1],
                                            accum_out=acc8[:, (qt * 2 + j) * 10 + kc8:
                                                           (qt * 2 + j) * 10 + kc8 + 1]),
                                        waits=wx, first=(kc8 == 0), last=(kc8 == 7))
                            exp_tok[(qt, j, kc8)] = tk
                        if qt == 1 and kc8 >= 1:
                            for kc in range(4 * (kc8 - 1), 4 * kc8):
                                emit_pt(kc)
                                if kc >= 1:
                                    emit_pv(kc - 1)
                    if qt == 1:
                        for kc in range(28, 32):
                            emit_pt(kc)
                            emit_pv(kc - 1)
                        emit_pv(31)
                    for j2 in range(2):
                        sp = s_P[qt * 2 + j2]
                        sa = s_acc[qt * 2 + j2]
                        sa.prod = sp.prod
                        sa.cons = []
                        g9 = (qt * 2 + j2) * 10
                        tsu = c.consume([sa], "v", lambda e, g9=g9: e.reduce_sum(
                            acc8[:, g9 + 8:g9 + 9], acc8[:, g9:g9 + 8], axis=AX.X))
                        tsc = c.consume([sa], "g", lambda e, g9=g9: e.tensor_copy(
                            acc8[:, g9 + 9:g9 + 10], acc8[:, g9 + 8:g9 + 9]),
                            waits=(tsu,))
                        rc = iq * 4 + qt * 2 + j2
                        c.xfer([sa], s_rin[rc], "v",
                               lambda e, rc=rc, g9=g9: e.reciprocal(
                                   rinv[:, rc:rc + 1],
                                   acc8[:, g9 + 9:g9 + 10]), waits=(tsc,))
                def onorm(p=p, qs=qs, iq=iq):
                    c.xfer([s_psO], s_Oraw, "v",
                           lambda e: e.tensor_copy(Oraw[:], psO[:]))
                    for qt in range(2):
                        c.xfer([s_Oraw], s_psPT[0][qt], "t",
                               lambda e, qt=qt: e.transpose(
                                   psPTb(0, qt)[:, 0:128],
                                   Oraw[:, qt * 128:(qt + 1) * 128], eye_sb[:]))
                        def fsc(e, qt=qt, iq=iq):
                            for j in range(2):
                                rc = iq * 4 + qt * 2 + j
                                inst = e.tensor_scalar_mul(
                                    Otok[:, qt * 128 + j * 64: qt * 128 + (j + 1) * 64],
                                    psPTb(0, qt)[:, j * 64:(j + 1) * 64],
                                    rinv[:, rc:rc + 1])
                            return inst
                        c.xfer([s_psPT[0][qt], s_rin[iq * 4 + qt * 2],
                                s_rin[iq * 4 + qt * 2 + 1]],
                               s_Otk[qt], "v", fsc)
                        c.xfer([s_Otk[qt]], s_psPT[1][qt], "t",
                               lambda e, qt=qt: e.transpose(
                                   psPTb(1, qt)[:, 0:128],
                                   Otok[:, qt * 128:(qt + 1) * 128], eye_sb[:]))
                        c.xfer([s_psPT[1][qt]], s_OTq[qt], "v",
                               lambda e, qt=qt: e.tensor_copy(
                                   OTst[:, qt * 128:(qt + 1) * 128],
                                   psPTb(1, qt)[:, 0:128]))
                    c.consume([s_OTq[0], s_OTq[1]], "g", lambda e, p=p, qs=qs:
                              e.dma_start(
                                  out=dt["OT_s"][p * 128:(p + 1) * 128,
                                                 qs * 256:(qs + 1) * 256],
                                  in_=OTst[:]), inc="d")
                pend_onorm[0] = onorm
        if pend_onorm[0] is not None:
            pend_onorm[0]()
            pend_onorm[0] = None
        b.barrier()
        esB.close()
        if finish(3):
            return nc, gstack, dt

        esC = ExitStack()
        wo_sb = esC.enter_context(nc.sbuf_tensor("wo_sb", [128, 8 * 1024], F32))
        ot_sl = esC.enter_context(nc.sbuf_tensor("ot_sl", [128, 2 * 4096], F32))
        at_sl = esC.enter_context(nc.sbuf_tensor("at_sl", [128, 4096], F32))
        a_tok = esC.enter_context(nc.sbuf_tensor("a_tok", [128, 2 * 1024], F32))
        x0_sl = esC.enter_context(nc.sbuf_tensor("x0_sl", [128, 2 * 4 * 1024], F32))
        lnout = esC.enter_context(nc.sbuf_tensor("lnoutC", [128, 1024], F32))
        xnew = esC.enter_context(nc.sbuf_tensor("xnewC", [128, 2 * 1024], F32))
        lnC = LNKit(nc, esC, "C")
        psA = [esC.enter_context(nc.psum_tensor(f"psC{i}", [128, 512], F32))
               for i in range(2)]
        psT = [esC.enter_context(nc.psum_tensor(f"psCT{i}", [128, 512], F32))
               for i in range(2)]

        s_wo = Slot()
        s_ot = slots(2)
        s_at = slots(8)
        s_atk = [_attach(Slot(), a_tok[:, i * 1024:(i + 1) * 1024]) for i in range(2)]
        s_x0 = [[_attach(Slot(), x0_sl[:, (i * 4 + tt) * 1024:(i * 4 + tt + 1) * 1024])
                 for tt in range(4)] for i in range(2)]
        s_ln = _attach(Slot(), lnout[:])
        s_xn2 = [_attach(Slot(), xnew[:, i * 1024:(i + 1) * 1024]) for i in range(2)]
        s_psA = slots(2)
        s_psT = slots(2)

        def fwo(e):
            for dc in range(NDC):
                inst = e.dma_start(out=wo_sb[:, dc * 1024:(dc + 1) * 1024],
                                   in_=dt["wout"][dc * 128:(dc + 1) * 128, :])
            return inst
        c.produce(s_wo, "y", fwo, inc="d")

        def loadC(ts):
            i = ts % 2
            def fot(e, ts=ts, i=i):
                for dc in range(NDC):
                    inst = e.dma_start(
                        out=ot_sl[:, i * 4096 + dc * 512: i * 4096 + (dc + 1) * 512],
                        in_=dt["OT_s"][dc * 128:(dc + 1) * 128, ts * 512:(ts + 1) * 512])
                return inst
            c.produce(s_ot[i], "y", fot, inc="d")
            for tt in range(4):
                c.produce(s_x0[i][tt], "y", lambda e, ts=ts, tt=tt, i=i: e.dma_start(
                    out=s_x0[i][tt].ap,
                    in_=dt["x_in"][ts * 512 + tt * 128:ts * 512 + (tt + 1) * 128, :]),
                    inc="d")

        loadC(0)
        for ts in range(NTS):
            i = ts % 2
            if ts + 1 < NTS:
                loadC(ts + 1)
            for oc in range(8):
                def fmm(e, oc=oc, i=i):
                    for dc in range(NDC):
                        inst = e.matmul(
                            psA[oc % 2][:],
                            wo_sb[:, dc * 1024 + oc * 128: dc * 1024 + (oc + 1) * 128],
                            ot_sl[:, i * 4096 + dc * 512: i * 4096 + (dc + 1) * 512],
                            start=(dc == 0), stop=(dc == NDC - 1))
                    return inst
                c.xfer([s_ot[i], s_wo], s_psA[oc % 2], "t", fmm)
                c.xfer([s_psA[oc % 2]], s_at[oc], "v",
                       lambda e, oc=oc: e.tensor_copy(
                           at_sl[:, oc * 512:(oc + 1) * 512], psA[oc % 2][:]))
            for tt in range(4):
                it = tt % 2
                for half in range(2):
                    def ftr2(e, tt=tt, it=it, half=half):
                        for o4 in range(4):
                            oc = half * 4 + o4
                            inst = e.transpose(
                                psT[it][:, o4 * 128:(o4 + 1) * 128],
                                at_sl[:, oc * 512 + tt * 128: oc * 512 + (tt + 1) * 128],
                                eye_sb[:])
                        return inst
                    c.xfer(s_at, s_psT[it], "t", ftr2, first=True, last=True)
                    c.xfer([s_psT[it]], s_atk[it], "v",
                           lambda e, it=it, half=half: e.tensor_copy(
                               a_tok[:, it * 1024 + half * 512: it * 1024 + (half + 1) * 512],
                               psT[it][:]),
                           first=(half == 0), last=(half == 1))
                lnC.run(c, s_atk[it], s_ln.ap, s_ln)
                c.xfer([s_ln, s_x0[i][tt]], s_xn2[it], "v",
                       lambda e, it=it, i=i, tt=tt: e.tensor_add(
                           s_xn2[it].ap, s_x0[i][tt].ap, lnout[:]))
                c.consume([s_xn2[it]], "s", lambda e, ts=ts, tt=tt, it=it: e.dma_start(
                    out=dt["x_s"][ts * 512 + tt * 128:ts * 512 + (tt + 1) * 128, :],
                    in_=s_xn2[it].ap), inc="d")
        b.barrier()
        esC.close()
        if finish(4):
            return nc, gstack, dt

        esD = ExitStack()
        xnTf = esD.enter_context(nc.sbuf_tensor("xnTf", [128, 32 * 1024], F32))
        xhD = esD.enter_context(nc.sbuf_tensor("xhD", [128, 2 * 1024], BF16))
        xlD = esD.enter_context(nc.sbuf_tensor("xlD", [128, 2 * 1024], BF16))
        xD_sl = esD.enter_context(nc.sbuf_tensor("xD_sl", [128, 2 * 1024], F32))
        xnD = esD.enter_context(nc.sbuf_tensor("xnD", [128, 2 * 1024], F32))
        rw_sb = esD.enter_context(nc.sbuf_tensor("rw_sb", [128, 8 * 64], F32))
        rb_sb = esD.enter_context(nc.sbuf_tensor("rb_sb", [1, 64], F32))
        rx = esD.enter_context(nc.sbuf_tensor("rxD", [128, 2 * 64], F32))
        rcol = esD.enter_context(nc.sbuf_tensor("rcolD", [128, 2 * 16], F32))
        mkD = esD.enter_context(nc.sbuf_tensor("mkD", [128, 2 * 128], F32))
        cwst = esD.enter_context(nc.sbuf_tensor("cwstD", [64, 2 * 128], F32))
        lnD = LNKit(nc, esD, "D")
        psD = [esD.enter_context(nc.psum_tensor(f"psD{i}", [128, 512], F32))
               for i in range(2)]
        psR = [esD.enter_context(nc.psum_tensor(f"psR{i}", [128, 64], F32))
               for i in range(2)]

        s_xD = [_attach(Slot(), xD_sl[:, i * 1024:(i + 1) * 1024]) for i in range(2)]
        s_xnD = [_attach(Slot(), xnD[:, i * 1024:(i + 1) * 1024]) for i in range(2)]
        s_xnTf = [Slot() for _ in range(NT)]
        s_xhD = slots(2)
        s_psD = slots(2)
        s_rw = Slot()
        s_psR = slots(2)
        s_rx = slots(2)
        s_rcol = slots(2)
        s_mk = slots(2)
        s_cwst = slots(2)

        def frw(e):
            for dc in range(NDC):
                e.dma_start(out=rw_sb[:, dc * 64:(dc + 1) * 64],
                            in_=dt["r_w"][dc * 128:(dc + 1) * 128, :])
            return e.dma_start(out=rb_sb[:], in_=dt["r_b"][:, :])
        c.produce(s_rw, "y", frw, inc="d")

        def loadD(tt):
            i = tt % 2
            c.produce(s_xD[i], "y", lambda e, tt=tt, i=i: e.dma_start(
                out=s_xD[i].ap, in_=dt["x_s"][tt * 128:(tt + 1) * 128, :]), inc="d")

        loadD(0)
        for tt in range(NT):
            i = tt % 2
            if tt + 1 < NT:
                loadD(tt + 1)
            lnD.run(c, s_xD[i], s_xnD[i].ap, s_xnD[i])
            for half in range(2):
                def ftr(e, tt=tt, i=i, half=half):
                    for d4 in range(4):
                        dc = half * 4 + d4
                        inst = e.transpose(
                            psD[i][:, d4 * 128:(d4 + 1) * 128],
                            xnD[:, i * 1024 + dc * 128: i * 1024 + (dc + 1) * 128],
                            eye_sb[:])
                    return inst
                c.xfer([s_xnD[i]], s_psD[i], "t", ftr)
                def fev(e, tt=tt, half=half, i=i):
                    for d4 in range(4):
                        dc = half * 4 + d4
                        inst = e.tensor_copy(
                            xnTf[:, tt * 1024 + dc * 128: tt * 1024 + (dc + 1) * 128],
                            psD[i][:, d4 * 128:(d4 + 1) * 128])
                    return inst
                c.xfer([s_psD[i]], s_xnTf[tt], "v", fev,
                       first=(half == 0), last=(half == 1))
            def fcast(e, tt=tt, i=i):
                e.tensor_copy(xhD[:, i * 1024:(i + 1) * 1024],
                              xnTf[:, tt * 1024:(tt + 1) * 1024])
                return e.tensor_tensor(xlD[:, i * 1024:(i + 1) * 1024],
                                       xnTf[:, tt * 1024:(tt + 1) * 1024],
                                       xhD[:, i * 1024:(i + 1) * 1024], ALU.subtract)
            c.xfer([s_xnTf[tt]], s_xhD[i], "v", fcast)
            def fsx(e, tt=tt, i=i):
                for dc in range(NDC):
                    e.dma_start(
                        out=dt["xh_s"][dc * 128:(dc + 1) * 128, tt * 128:(tt + 1) * 128],
                        in_=xhD[:, i * 1024 + dc * 128: i * 1024 + (dc + 1) * 128])
                    inst = e.dma_start(
                        out=dt["xl_s"][dc * 128:(dc + 1) * 128, tt * 128:(tt + 1) * 128],
                        in_=xlD[:, i * 1024 + dc * 128: i * 1024 + (dc + 1) * 128])
                return inst
            c.consume([s_xhD[i]], "g", fsx, inc="d")
        for tt in range(NT):
            i = tt % 2
            def fmm(e, tt=tt, i=i):
                for dc in range(NDC):
                    e.matmul(psR[i][:],
                             xnTf[:, tt * 1024 + dc * 128: tt * 1024 + (dc + 1) * 128],
                             rw_sb[:, dc * 64:(dc + 1) * 64],
                             start=(dc == 0), stop=False)
                return e.matmul(psR[i][:], ones[0:1, 0:128], rb_sb[:],
                                start=False, stop=True)
            c.xfer([s_xnTf[tt], s_rw], s_psR[i], "t", fmm)
            ex = rx[:, i * 64:(i + 1) * 64]
            cl = rcol[:, i * 16:(i + 1) * 16]
            msk = mkD[:, i * 128:i * 128 + 64]
            cwm = mkD[:, i * 128 + 64:i * 128 + 128]
            (mx, ngm, m8, thr, s8, s8r, s8c) = (
                cl[:, 0:1], cl[:, 1:2], cl[:, 2:10],
                cl[:, 10:11], cl[:, 11:12], cl[:, 12:13], cl[:, 13:14])
            t1a = c.xfer([s_psR[i]], s_rcol[i], "v",
                         lambda e, i=i, mx=mx: e.reduce_max(mx, psR[i][:], axis=AX.X))
            t1 = c.produce(s_rcol[i], "g",
                           lambda e, mx=mx, ngm=ngm: e.tensor_scalar_mul(ngm, mx, -1.0),
                           waits=(t1a,), first=False)
            t2 = c.xfer([s_psR[i], s_rcol[i]], s_rx[i], "s",
                        lambda e, i=i, ex=ex, ngm=ngm: e.activation(
                            ex, psR[i][:], AF.Exp, bias=ngm), waits=(t1,))
            t3a = c.produce(s_rcol[i], "v", lambda e, ex=ex, m8=m8: e.max(m8, ex),
                            waits=(t2,), first=False)
            t3 = c.produce(s_rcol[i], "g",
                           lambda e, m8=m8, thr=thr: e.tensor_copy(thr, m8[:, 7:8]),
                           waits=(t3a,), first=False)
            def fr3(e, ex=ex, msk=msk, cwm=cwm, thr=thr):
                ea, tb = broadcast_tensor_aps(ex, thr)
                e.tensor_sub(cwm, ea, tb)
                e.tensor_single_scalar(msk, cwm, 0.0, ALU.is_ge)
                return e.tensor_mul(cwm, ex, msk)
            wmk = s_mk[i].cons + [s_mk[i].prod]
            t5a = c.xfer([s_rx[i], s_rcol[i]], s_rcol[i], "v", fr3,
                         waits=tuple(wmk) + (t3,), first=False)
            t5b = c.produce(s_rcol[i], "v",
                            lambda e, cwm=cwm, s8=s8: e.reduce_sum(s8, cwm, axis=AX.X),
                            waits=(t5a,), first=False)
            t5c = c.produce(s_rcol[i], "g",
                            lambda e, s8=s8, s8c=s8c: e.tensor_copy(s8c, s8),
                            waits=(t5b,), first=False)
            t5 = c.produce(s_rcol[i], "v",
                           lambda e, s8c=s8c, s8r=s8r: e.reciprocal(s8r, s8c),
                           waits=(t5c,), first=False)
            s_mk[i].prod = t5
            s_mk[i].cons = []
            t6 = c.xfer([s_mk[i], s_rcol[i]], s_rx[i], "g",
                        lambda e, ex=ex, cwm=cwm, s8r=s8r: e.tensor_scalar_mul(
                            ex, cwm, s8r), waits=(t5,))
            c.xfer([s_rx[i]], s_psD[i], "t",
                   lambda e, i=i, ex=ex: e.transpose(psD[i][0:64, 0:128], ex, eye_sb[:]))
            c.xfer([s_psD[i]], s_cwst[i], "v",
                   lambda e, i=i: e.tensor_copy(cwst[:, i * 128:(i + 1) * 128],
                                                psD[i][0:64, 0:128]))
            c.consume([s_cwst[i]], "g", lambda e, tt=tt, i=i: e.dma_start(
                out=dt["cw_s"][:, tt * 128:(tt + 1) * 128],
                in_=cwst[:, i * 128:(i + 1) * 128]), inc="d")
        b.barrier()
        esD.close()
        if finish(5):
            return nc, gstack, dt

        esE = ExitStack()
        wg1 = [esE.enter_context(nc.sbuf_tensor(f"wg1_{i}", [128, 4096], F32))
               for i in range(2)]
        wg2 = [esE.enter_context(nc.sbuf_tensor(f"wg2_{i}", [128, 4096], F32))
               for i in range(2)]
        w1h = esE.enter_context(nc.sbuf_tensor("w1h", [128, 8192], BF16))
        w1l = esE.enter_context(nc.sbuf_tensor("w1l", [128, 8192], BF16))
        w2h = esE.enter_context(nc.sbuf_tensor("w2h", [128, 8192], BF16))
        w2l = esE.enter_context(nc.sbuf_tensor("w2l", [128, 8192], BF16))
        xhE = esE.enter_context(nc.sbuf_tensor("xhE", [128, 2 * 4096], BF16))
        xlE = esE.enter_context(nc.sbuf_tensor("xlE", [128, 4096], BF16))
        H_sb = esE.enter_context(nc.sbuf_tensor("H_sb", [128, 2 * 512], F32))
        Hh = esE.enter_context(nc.sbuf_tensor("Hh", [128, 8 * 512], BF16))
        Hl = esE.enter_context(nc.sbuf_tensor("Hl", [128, 8 * 512], BF16))
        accE = esE.enter_context(nc.sbuf_tensor("accE", [128, 4 * 1024], F32))
        cwr = esE.enter_context(nc.sbuf_tensor("cwrE", [1, 4 * 512], F32))
        cwl = esE.enter_context(nc.sbuf_tensor("cwlE", [64, 512], F32))
        b1_sb = esE.enter_context(nc.sbuf_tensor("b1E", [128, 128], F32))
        sb1_sb = esE.enter_context(nc.sbuf_tensor("sb1E", [128, 32], F32))
        b2_sb = esE.enter_context(nc.sbuf_tensor("b2E", [64, 1024], F32))
        sb2_sb = esE.enter_context(nc.sbuf_tensor("sb2E", [1, 1024], F32))
        psH = [esE.enter_context(nc.psum_tensor(f"psH{i}", [128, 512], F32))
               for i in range(2)]
        psCW = [esE.enter_context(nc.psum_tensor(f"psCW{i}", [128, 512], F32))
                for i in range(2)]
        psM = [esE.enter_context(nc.psum_tensor(f"psM{i}", [128, 512], F32))
               for i in range(2)]

        s_wg1 = slots(2)
        s_wg2 = slots(2)
        s_w1b = slots(2)
        s_w2b = slots(2)
        s_xE = slots(2)
        s_xl = Slot()
        s_H = slots(2)
        s_Hb = slots(8)
        s_accT = [[Slot() for _ in range(4)] for _ in range(1)]
        s_accD = slots(NTS)
        s_cwr = slots(4)
        s_cwl = slots(2)
        s_psH = slots(2)
        s_psCW = slots(2)
        s_psM = slots(2)
        s_cst = Slot()
        s_sb2 = Slot()

        def fc(e):
            e.dma_start(out=b1_sb[:], in_=dt["b1T"][:, :])
            e.dma_start(out=sb1_sb[:], in_=dt["sb1T"][:, :])
            return e.dma_start(out=b2_sb[:], in_=dt["b2"][:, :])
        c.produce(s_cst, "y", fc, inc="d")

        NGRP = 36

        def grp_info(g):
            if g < 32:
                return ("r", [2 * g, 2 * g + 1], 4, 0)
            se, half = (g - 32) // 2, (g - 32) % 2
            return ("s", [se], 8, half * 8)

        def cast_w(i):
            def f1(e, i=i):
                e.tensor_copy(w1h[:, i * 4096:(i + 1) * 4096], wg1[i][:])
                return e.tensor_tensor(w1l[:, i * 4096:(i + 1) * 4096], wg1[i][:],
                                       w1h[:, i * 4096:(i + 1) * 4096], ALU.subtract)
            c.xfer([s_wg1[i]], s_w1b[i], "v", f1)
            def f2(e, i=i):
                e.tensor_copy(w2h[:, i * 4096:(i + 1) * 4096], wg2[i][:])
                return e.tensor_tensor(w2l[:, i * 4096:(i + 1) * 4096], wg2[i][:],
                                       w2h[:, i * 4096:(i + 1) * 4096], ALU.subtract)
            c.xfer([s_wg2[i]], s_w2b[i], "v", f2)

        def load_group(g):
            kind, exps, nhc, hoff = grp_info(g)
            i = g % 2
            if kind == "r":
                def f1(e, exps=exps, i=i):
                    for k, ex_ in enumerate(exps):
                        for dc in range(NDC):
                            inst = e.dma_start(
                                out=wg1[i][:, (k * 8 + dc) * 256:(k * 8 + dc + 1) * 256],
                                in_=dt["w1"][ex_, dc * 128:(dc + 1) * 128, :])
                    return inst
                def f2(e, exps=exps, i=i):
                    for k, ex_ in enumerate(exps):
                        for hc in range(2):
                            inst = e.dma_start(
                                out=wg2[i][:, (k * 2 + hc) * 1024:(k * 2 + hc + 1) * 1024],
                                in_=dt["w2"][ex_, hc * 128:(hc + 1) * 128, :])
                    return inst
                c.produce(s_wg1[i], "y", f1, inc="d")
                c.produce(s_wg2[i], "y", f2, inc="d")
                cast_w(i)
            else:
                se = exps[0]
                def f1(e, se=se, hoff=hoff):
                    for dc in range(NDC):
                        idx = dc * 1024
                        inst = e.dma_start(
                            out=wg1[idx // 4096][:, idx % 4096: idx % 4096 + 1024],
                            in_=dt["sw1"][se, dc * 128:(dc + 1) * 128,
                                          hoff * 128: hoff * 128 + 1024])
                    return inst
                def f2(e, se=se, hoff=hoff):
                    for hc in range(8):
                        idx = hc * 1024
                        inst = e.dma_start(
                            out=wg2[idx // 4096][:, idx % 4096: idx % 4096 + 1024],
                            in_=dt["sw2"][se, (hoff + hc) * 128:(hoff + hc + 1) * 128, :])
                    return inst
                tok1 = b.emit("y", f1, waits=s_wg1[0].cons + s_wg1[1].cons +
                              [s_wg1[0].prod, s_wg1[1].prod], inc="d",
                              grp=id(s_wg1[0]))
                s_wg1[0].prod = s_wg1[1].prod = tok1
                s_wg1[0].cons = []
                s_wg1[1].cons = []
                tok2 = b.emit("y", f2, waits=s_wg2[0].cons + s_wg2[1].cons +
                              [s_wg2[0].prod, s_wg2[1].prod], inc="d",
                              grp=id(s_wg2[0]))
                s_wg2[0].prod = s_wg2[1].prod = tok2
                s_wg2[0].cons = []
                s_wg2[1].cons = []
                cast_w(0)
                cast_w(1)
                if hoff == 0:
                    c.produce(s_sb2, "y", lambda e, se=se: e.dma_start(
                        out=sb2_sb[:], in_=dt["sb2r"][0:1, se * 1024:(se + 1) * 1024]),
                        inc="d")

        def w1_tile(wt, g, k, dc, hc):
            if g < 32:
                base = (g % 2) * 4096 + (k * 8 + dc) * 256 + hc * 128
                return wt[:, base:base + 128]
            idx = dc * 1024 + hc * 128
            return wt[:, idx:idx + 128]

        def w2_tile(wt, g, k, hc, dc2):
            if g < 32:
                base = (g % 2) * 4096 + (k * 2 + hc) * 1024 + dc2 * 512
                return wt[:, base:base + 512]
            idx = hc * 1024 + dc2 * 512
            return wt[:, idx:idx + 512]

        load_group(0)
        for g in range(NGRP):
            kind, exps, nhc, hoff = grp_info(g)
            ig = g % 2
            prefetch_now = (g + 1 < NGRP and kind == "r"
                            and grp_info(g + 1)[0] == "r")
            if prefetch_now:
                load_group(g + 1)
            for ts in range(NTS):
                it = ts % 2
                def fx(e, ts=ts, it=it):
                    for dc in range(NDC):
                        inst = e.dma_start(
                            out=xhE[:, it * 4096 + dc * 512: it * 4096 + (dc + 1) * 512],
                            in_=dt["xh_s"][dc * 128:(dc + 1) * 128,
                                           ts * 512:(ts + 1) * 512])
                    return inst
                c.produce(s_xE[it], "y", fx, inc="d")
                def fxl(e, ts=ts):
                    for dc in range(NDC):
                        inst = e.dma_start(
                            out=xlE[:, dc * 512:(dc + 1) * 512],
                            in_=dt["xl_s"][dc * 128:(dc + 1) * 128,
                                           ts * 512:(ts + 1) * 512])
                    return inst
                c.produce(s_xl, "y", fxl, inc="d")
                if g > 0:
                    def fa(e, ts=ts, it=it):
                        for tt in range(4):
                            inst = e.dma_start(
                                out=accE[:, tt * 1024:(tt + 1) * 1024],
                                in_=dt["acc_s"][ts * 512 + tt * 128:
                                                ts * 512 + (tt + 1) * 128, :])
                        return inst
                    wl = [s_accD[ts].prod]
                    for tt2 in range(4):
                        wl += s_accT[0][tt2].cons + [s_accT[0][tt2].prod]
                    tok = b.emit("y", fa, waits=wl, inc="d",
                                 grp=id(s_accT[0][0]))
                    for tt in range(4):
                        s_accT[0][tt].prod = tok
                        s_accT[0][tt].cons = []
                    s_accD[ts].cons.append(tok)
                def emit_mm1(hcol, g, k, hc, bias_ap, cwslot, cwbank):
                    ih = hcol % 2
                    def fm1(e, g=g, k=k, hc=hc, it=it, ih=ih):
                        n = 0
                        for dc in range(NDC):
                            for ph in range(3):
                                wt = (w1h, w1l, w1h)[ph]
                                xo = it * 4096 if ph < 2 else 0
                                xt = (xhE, xhE, xlE)[ph]
                                inst = e.matmul(
                                    psH[ih][:], w1_tile(wt, g, k, dc, hc),
                                    xt[:, xo + dc * 512: xo + (dc + 1) * 512],
                                    start=(n == 0), stop=(n == 23))
                                n += 1
                        return inst
                    srcs = [s_xE[it], s_xl, s_w1b[ig]] if g < 32 else \
                           [s_xE[it], s_xl, s_w1b[0], s_w1b[1]]
                    c.xfer(srcs, s_psH[ih], "t", fm1)
                    hs = hcol % 2
                    c.xfer([s_psH[ih], s_cst], s_H[hs], "s",
                           lambda e, ih=ih, hs=hs, bias_ap=bias_ap:
                           e.activation(H_sb[:, hs * 512:(hs + 1) * 512],
                                        psH[ih][:], AF.Silu, bias=bias_ap))
                    if cwslot is not None:
                        c.xfer([cwslot, s_H[hs]], s_H[hs], "v",
                               lambda e, hs=hs, cwbank=cwbank: e.tensor_mul(
                                   H_sb[:, hs * 512:(hs + 1) * 512],
                                   H_sb[:, hs * 512:(hs + 1) * 512],
                                   cwbank[:]),
                               first=False, last=True)
                    def fhc(e, hcol=hcol, hs=hs):
                        e.tensor_copy(Hh[:, hcol * 512:(hcol + 1) * 512],
                                      H_sb[:, hs * 512:(hs + 1) * 512])
                        return e.tensor_tensor(
                            Hl[:, hcol * 512:(hcol + 1) * 512],
                            H_sb[:, hs * 512:(hs + 1) * 512],
                            Hh[:, hcol * 512:(hcol + 1) * 512], ALU.subtract)
                    c.xfer([s_H[hs], s_Hb[hcol]], s_Hb[hcol], "v", fhc)

                if kind == "r":
                    def load_cw(tsx):
                        for k2, ex2 in enumerate(exps):
                            ic2 = (tsx * 2 + k2) % 4
                            c.produce(s_cwr[ic2], "s",
                                      lambda e, ex2=ex2, tsx=tsx, ic2=ic2:
                                      e.dma_start(
                                          out=cwr[:, ic2 * 512:(ic2 + 1) * 512],
                                          in_=dt["cw_s"][ex2:ex2 + 1,
                                                         tsx * 512:(tsx + 1) * 512]),
                                      inc="d")
                    if ts == 0:
                        load_cw(0)
                        load_cw(1)
                    elif ts + 1 < NTS:
                        load_cw(ts + 1)
                    for k, ex_ in enumerate(exps):
                        ic = (ts * 2 + k) % 4
                        c.xfer([s_cwr[ic]], s_psCW[ic % 2], "t",
                               lambda e, ic=ic: e.matmul(
                                   psCW[ic % 2][:], ones[0:1, 0:128],
                                   cwr[:, ic * 512:(ic + 1) * 512],
                                   start=True, stop=True))
                        for hc in range(2):
                            emit_mm1(k * 2 + hc, g, k, hc,
                                     b1_sb[:, hc * 64 + ex_:hc * 64 + ex_ + 1],
                                     s_psCW[ic % 2], psCW[ic % 2])
                else:
                    se = exps[0]
                    for hc in range(8):
                        emit_mm1(hc, g, 0, hc,
                                 sb1_sb[:, se * 16 + hoff + hc:se * 16 + hoff + hc + 1],
                                 None, None)
                if g == NGRP - 1:
                    c.produce(s_cwl[0], "y", lambda e, ts=ts: e.dma_start(
                        out=cwl[:, 0:512],
                        in_=dt["cw_s"][:, ts * 512:(ts + 1) * 512]), inc="d")
                sb2_here = (kind == "s" and hoff == 0)
                b2_here = (g == NGRP - 1)
                for tt in range(4):
                    for dc2 in range(2):
                        im = (tt * 2 + dc2) % 2
                        tot = nhc * 3 + (1 if sb2_here else 0) + (1 if b2_here else 0)
                        def fm2(e, g=g, tt=tt, dc2=dc2, it=it, im=im, nhc=nhc,
                                kind=kind, exps=exps, tot=tot,
                                sb2_here=sb2_here, b2_here=b2_here):
                            n = 0
                            for hcol in range(nhc):
                                k, hc = ((hcol // 2, hcol % 2) if kind == "r"
                                         else (0, hcol))
                                for ph in range(3):
                                    ht = (Hh, Hl, Hh)[ph]
                                    wt = (w2h, w2h, w2l)[ph]
                                    inst = e.matmul(
                                        psM[im][:],
                                        ht[:, hcol * 512 + tt * 128:
                                           hcol * 512 + (tt + 1) * 128],
                                        w2_tile(wt, g, k, hc, dc2),
                                        start=(n == 0), stop=(n == tot - 1))
                                    n += 1
                            if sb2_here:
                                inst = e.matmul(
                                    psM[im][:], ones[0:1, 0:128],
                                    sb2_sb[0:1, dc2 * 512:(dc2 + 1) * 512],
                                    start=False, stop=(n == tot - 1))
                                n += 1
                            if b2_here:
                                inst = e.matmul(
                                    psM[im][:],
                                    cwl[:, tt * 128:(tt + 1) * 128],
                                    b2_sb[:, dc2 * 512:(dc2 + 1) * 512],
                                    start=False, stop=True)
                            return inst
                        srcs = list(s_Hb[:nhc])
                        if kind == "r":
                            srcs.append(s_w2b[ig])
                        else:
                            srcs += [s_w2b[0], s_w2b[1]]
                        if sb2_here:
                            srcs.append(s_sb2)
                        if b2_here:
                            srcs.append(s_cwl[0])
                        c.xfer(srcs, s_psM[im], "t", fm2)
                        aap = accE[:, tt * 1024 + dc2 * 512:
                                   tt * 1024 + (dc2 + 1) * 512]
                        if g == 0:
                            c.xfer([s_psM[im]], s_accT[0][tt], "v",
                                   lambda e, aap=aap, im=im: e.tensor_copy(
                                       aap, psM[im][:]),
                                   first=(dc2 == 0), last=(dc2 == 1))
                        else:
                            c.xfer([s_psM[im]], s_accT[0][tt], "v",
                                   lambda e, aap=aap, im=im: e.tensor_add(
                                       aap, aap, psM[im][:]),
                                   first=(dc2 == 0), last=(dc2 == 1))
                def fst(e, ts=ts, it=it):
                    for tt in range(4):
                        inst = e.dma_start(
                            out=dt["acc_s"][ts * 512 + tt * 128:ts * 512 + (tt + 1) * 128, :],
                            in_=accE[:, tt * 1024:(tt + 1) * 1024])
                    return inst
                tok = b.emit("s", fst, waits=[s_accT[0][tt].prod for tt in range(4)] +
                             s_accD[ts].cons + [s_accD[ts].prod], inc="d",
                             grp=id(s_accD[ts]))
                for tt in range(4):
                    s_accT[0][tt].cons.append(tok)
                s_accD[ts].prod = tok
                s_accD[ts].cons = []
            if g + 1 < NGRP and not prefetch_now:
                load_group(g + 1)
        b.barrier()
        esE.close()
        if finish(6):
            return nc, gstack, dt

        esF = ExitStack()
        aF = esF.enter_context(nc.sbuf_tensor("aF", [128, 2 * 1024], F32))
        xF = esF.enter_context(nc.sbuf_tensor("xF", [128, 2 * 1024], F32))
        lnF_o = esF.enter_context(nc.sbuf_tensor("lnFo", [128, 1024], F32))
        x2F = esF.enter_context(nc.sbuf_tensor("x2F", [128, 2 * 1024], F32))
        lnF = LNKit(nc, esF, "F")
        s_aF = [_attach(Slot(), aF[:, i * 1024:(i + 1) * 1024]) for i in range(2)]
        s_xF = [_attach(Slot(), xF[:, i * 1024:(i + 1) * 1024]) for i in range(2)]
        s_lnF = _attach(Slot(), lnF_o[:])
        s_x2F = [_attach(Slot(), x2F[:, i * 1024:(i + 1) * 1024]) for i in range(2)]

        def loadF(tt):
            i = tt % 2
            c.produce(s_aF[i], "y", lambda e, tt=tt, i=i: e.dma_start(
                out=s_aF[i].ap, in_=dt["acc_s"][tt * 128:(tt + 1) * 128, :]), inc="d")
            c.produce(s_xF[i], "y", lambda e, tt=tt, i=i: e.dma_start(
                out=s_xF[i].ap, in_=dt["x_s"][tt * 128:(tt + 1) * 128, :]), inc="d")

        loadF(0)
        for tt in range(NT):
            i = tt % 2
            if tt + 1 < NT:
                loadF(tt + 1)
            lnF.run(c, s_aF[i], s_lnF.ap, s_lnF)
            c.xfer([s_lnF, s_xF[i]], s_x2F[i], "v",
                   lambda e, i=i: e.tensor_add(s_x2F[i].ap, s_xF[i].ap, lnF_o[:]))
            c.consume([s_x2F[i]], "s", lambda e, tt=tt, i=i: e.dma_start(
                out=dt["x_s"][tt * 128:(tt + 1) * 128, :], in_=s_x2F[i].ap), inc="d")
        b.barrier()
        esF.close()
        if finish(7):
            return nc, gstack, dt

        esG = ExitStack()
        w3 = esG.enter_context(nc.sbuf_tensor("w3G", [128, 3 * 8 * 1024], F32))
        xG = esG.enter_context(nc.sbuf_tensor("xG", [128, 4 * 1024], F32))
        xnG = esG.enter_context(nc.sbuf_tensor("xnG", [128, 4 * 1024], F32))
        xnTG = esG.enter_context(nc.sbuf_tensor("xnTG", [128, 2 * 4096], F32))
        pTg = esG.enter_context(nc.sbuf_tensor("pTg", [128, 2 * 512], F32))
        ptok = esG.enter_context(nc.sbuf_tensor("ptokG", [128, 4 * 1024], F32))
        sqG = esG.enter_context(nc.sbuf_tensor("sqG", [128, 1024], F32))
        colG = esG.enter_context(nc.sbuf_tensor("colG", [128, 64], F32))
        qnT = esG.enter_context(nc.sbuf_tensor("qnTG", [128, 8 * 512], F32))
        lnG = LNKit(nc, esG, "G")
        psG = [esG.enter_context(nc.psum_tensor(f"psG{i}", [128, 512], F32))
               for i in range(2)]
        psG2 = [esG.enter_context(nc.psum_tensor(f"psG2{i}", [128, 512], F32))
                for i in range(2)]

        s_w3 = Slot()
        s_xG = [_attach(Slot(), xG[:, tt * 1024:(tt + 1) * 1024])
                for tt in range(4)]
        s_xnG = [_attach(Slot(), xnG[:, tt * 1024:(tt + 1) * 1024]) for tt in range(4)]
        s_xnTG = [[Slot() for _ in range(NDC)] for _ in range(2)]
        s_psG = slots(2)
        s_psG2 = slots(2)
        s_pT = slots(2)
        s_ptok = [_attach(Slot(), ptok[:, tt * 1024:(tt + 1) * 1024]) for tt in range(4)]
        s_sq = Slot()
        s_col = Slot()
        s_qnT = slots(8)

        def fw3(e):
            for w in range(3):
                nm = ("wq1T", "wk1T", "wv1T")[w]
                for dc in range(NDC):
                    inst = e.dma_start(
                        out=w3[:, (w * 8 + dc) * 1024:(w * 8 + dc + 1) * 1024],
                        in_=dt[nm][dc * 128:(dc + 1) * 128, :])
            return inst
        c.produce(s_w3, "y", fw3, inc="d")

        def loadG(ts):
            for tt in range(4):
                c.produce(s_xG[tt], "y", lambda e, ts=ts, tt=tt: e.dma_start(
                    out=s_xG[tt].ap,
                    in_=dt["x_s"][ts * 512 + tt * 128:ts * 512 + (tt + 1) * 128, :]),
                    inc="d")

        for ts in range(DEV.get("G_ts", NTS)):
            i = ts % 2
            loadG(ts)
            for tt in range(4):
                lnG.run(c, s_xG[tt], s_xnG[tt].ap, s_xnG[tt])
            for dc in range(NDC):
                def ftr(e, dc=dc):
                    for tt in range(4):
                        inst = e.transpose(
                            psG[dc % 2][:, tt * 128:(tt + 1) * 128],
                            xnG[:, tt * 1024 + dc * 128: tt * 1024 + (dc + 1) * 128],
                            eye_sb[:])
                    return inst
                c.xfer(s_xnG, s_psG[dc % 2], "t", ftr)
                c.xfer([s_psG[dc % 2]], s_xnTG[i][dc], "v",
                       lambda e, dc=dc, i=i: e.tensor_copy(
                           xnTG[:, i * 4096 + dc * 512: i * 4096 + (dc + 1) * 512],
                           psG[dc % 2][:]))
            for w in range(DEV.get("G_w", 3)):
                def emit_tr(ec):
                    pe = ec % 2
                    for tt in range(4):
                        c.xfer([s_pT[pe]], s_psG[tt % 2], "t",
                               lambda e, tt=tt, pe=pe: e.transpose(
                                   psG[tt % 2][:, (tt // 2) * 128:(tt // 2) * 128 + 128],
                                   pTg[:, pe * 512 + tt * 128: pe * 512 + (tt + 1) * 128],
                                   eye_sb[:]))
                        c.xfer([s_psG[tt % 2]], s_ptok[tt], "v",
                               lambda e, tt=tt, ec=ec: e.tensor_copy(
                                   ptok[:, tt * 1024 + ec * 128: tt * 1024 + (ec + 1) * 128],
                                   psG[tt % 2][:, (tt // 2) * 128:(tt // 2) * 128 + 128]),
                               first=(ec == 0), last=(ec == 7))
                for ec in range(8):
                    def fmm(e, w=w, ec=ec, i=i):
                        for dc in range(NDC):
                            inst = e.matmul(
                                psG2[ec % 2][:],
                                w3[:, (w * 8 + dc) * 1024 + ec * 128:
                                   (w * 8 + dc) * 1024 + (ec + 1) * 128],
                                xnTG[:, i * 4096 + dc * 512: i * 4096 + (dc + 1) * 512],
                                start=(dc == 0), stop=(dc == NDC - 1))
                        return inst
                    c.xfer(s_xnTG[i] + [s_w3], s_psG2[ec % 2], "t", fmm)
                    c.xfer([s_psG2[ec % 2]], s_pT[ec % 2], "v",
                           lambda e, ec=ec: e.tensor_copy(
                               pTg[:, (ec % 2) * 512:(ec % 2 + 1) * 512],
                               psG2[ec % 2][:]))
                    if ec >= 1:
                        emit_tr(ec - 1)
                emit_tr(7)
                for tt in range(4):
                    pa = s_ptok[tt].ap
                    if w < 2:
                        t1 = c.xfer([s_ptok[tt]], s_sq, "v",
                                    lambda e, pa=pa: e.tensor_mul(sqG[:], pa, pa))
                        def fss(e):
                            for h in range(16):
                                inst = e.reduce_sum(colG[:, h:h + 1],
                                                    sqG[:, h * 64:(h + 1) * 64],
                                                    axis=AX.X)
                            return inst
                        t2 = c.xfer([s_sq], s_col, "v", fss, waits=(t1,))
                        t3 = c.produce(s_col, "g",
                                       lambda e: e.tensor_scalar_max(
                                           colG[:, 16:32], colG[:, 0:16], 1e-24),
                                       waits=(t2,), first=False)
                        t4a = c.produce(s_col, "v",
                                        lambda e: e.reciprocal(colG[:, 32:48],
                                                               colG[:, 16:32]),
                                        waits=(t3,), first=False)
                        t4 = c.produce(s_col, "s",
                                       lambda e: e.activation(colG[:, 48:64],
                                                              colG[:, 32:48], AF.Sqrt),
                                       waits=(t4a,), first=False)
                        def fap(e, pa=pa):
                            for h in range(16):
                                inst = e.tensor_scalar_mul(
                                    pa[:, h * 64:(h + 1) * 64],
                                    pa[:, h * 64:(h + 1) * 64],
                                    colG[:, 48 + h:49 + h])
                            return inst
                        c.xfer([s_col, s_ptok[tt]], s_ptok[tt], "v", fap)
                    if w == 0:
                        for ec in range(8):
                            c.xfer([s_ptok[tt]], s_psG[ec % 2], "t",
                                   lambda e, ec=ec, pa=pa: e.transpose(
                                       psG[ec % 2][:, 0:128],
                                       pa[:, ec * 128:(ec + 1) * 128], eye_sb[:]))
                            c.xfer([s_psG[ec % 2]], s_qnT[ec], "v",
                                   lambda e, ec=ec, tt=tt: e.tensor_copy(
                                       qnT[:, ec * 512 + tt * 128: ec * 512 + (tt + 1) * 128],
                                       psG[ec % 2][:, 0:128]),
                                   first=(tt == 0), last=(tt == 3))
                    else:
                        dst = dt["k2_s"] if (w == 1 or DEV.get("G_vk")) else dt["v2_s"]
                        c.consume([s_ptok[tt]], "g",
                                  lambda e, dst=dst, ts=ts, tt=tt, pa=pa: e.dma_start(
                                      out=dst[ts * 512 + tt * 128:ts * 512 + (tt + 1) * 128, :],
                                      in_=pa), inc="d")
                if w == 0:
                    def fq(e, ts=ts):
                        for ec in range(8):
                            inst = e.dma_start(
                                out=dt["q2T_s"][ec * 128:(ec + 1) * 128,
                                                ts * 512:(ts + 1) * 512],
                                in_=qnT[:, ec * 512:(ec + 1) * 512])
                        return inst
                    c.consume(s_qnT, "g", fq, inc="d")
        b.barrier()
        esG.close()
        if finish(8):
            return nc, gstack, dt

        esH = ExitStack()
        fac_r = esH.enter_context(nc.sbuf_tensor("facH", [1, 64], F32))
        fac64 = esH.enter_context(nc.sbuf_tensor("fac64H", [64, 16], F32))
        kvH = esH.enter_context(nc.sbuf_tensor("kvH", [128, 8 * 64], F32))
        ktH = esH.enter_context(nc.sbuf_tensor("ktH", [128, 2 * 1024], F32))
        vtH = esH.enter_context(nc.sbuf_tensor("vtH", [128, 2 * 1024], F32))
        qnH = esH.enter_context(nc.sbuf_tensor("qnH", [128, 2 * 4096], F32))
        atH = esH.enter_context(nc.sbuf_tensor("atH", [128, 4096], F32))
        woH = esH.enter_context(nc.sbuf_tensor("woH", [128, 8 * 1024], F32))
        aTH = esH.enter_context(nc.sbuf_tensor("aTH", [128, 4096], F32))
        atokH = esH.enter_context(nc.sbuf_tensor("atokH", [128, 2 * 1024], F32))
        xH = esH.enter_context(nc.sbuf_tensor("xH", [128, 2 * 4 * 1024], F32))
        lnoH = esH.enter_context(nc.sbuf_tensor("lnoH", [128, 1024], F32))
        x3H = esH.enter_context(nc.sbuf_tensor("x3H", [128, 2 * 1024], F32))
        lnH = LNKit(nc, esH, "H")
        psKV = esH.enter_context(nc.psum_tensor("psKV", [64, 1024], F32))
        psAT = [esH.enter_context(nc.psum_tensor(f"psAT{i}", [64, 512], F32))
                for i in range(2)]
        psP = [esH.enter_context(nc.psum_tensor(f"psHP{i}", [128, 512], F32))
               for i in range(2)]
        psF = esH.enter_context(nc.psum_tensor("psHF", [64, 16], F32))

        s_facr = Slot()
        s_fac = Slot()
        s_kv = Slot()
        s_kvsb = Slot()
        s_kt = slots(2)
        s_vt = slots(2)
        s_qn = slots(2)
        s_at = slots(8)
        s_wo = Slot()
        s_psAT = slots(2)
        s_psP = slots(2)
        s_psF = Slot()
        s_aT = slots(8)
        s_atok = [_attach(Slot(), atokH[:, i * 1024:(i + 1) * 1024]) for i in range(2)]
        s_xH = [[_attach(Slot(), xH[:, (i * 4 + tt) * 1024:(i * 4 + tt + 1) * 1024])
                 for tt in range(4)] for i in range(2)]
        s_lnH = _attach(Slot(), lnoH[:])
        s_x3 = [_attach(Slot(), x3H[:, i * 1024:(i + 1) * 1024]) for i in range(2)]

        t_dm = b.emit("y", lambda e: e.dma_start(out=fac_r[0:1, 0:16],
                                                 in_=dt["ncst"][:, :]), inc="d")
        t_ms = b.emit("v", lambda e: e.memset(fac_r[0:1, 32:48], 4096.0), inc="c")
        t_sg = b.emit("s", lambda e: e.activation(fac_r[0:1, 16:32], fac_r[0:1, 0:16],
                                                  AF.Sigmoid), waits=(t_dm,), inc="c")
        t_ln = b.emit("s", lambda e: e.activation(fac_r[0:1, 32:48], fac_r[0:1, 32:48],
                                                  AF.Ln), waits=(t_ms,), inc="c")
        def ffin(e):
            e.tensor_mul(fac_r[0:1, 48:64], fac_r[0:1, 16:32], fac_r[0:1, 32:48])
            return e.tensor_scalar_mul(fac_r[0:1, 0:16], fac_r[0:1, 48:64], -1.0)
        t_mu = b.emit("g", ffin, waits=(t_sg, t_ln), inc="c")
        t_ex = b.emit("s", lambda e: e.activation(fac_r[0:1, 16:32], fac_r[0:1, 0:16],
                                                  AF.Exp), waits=(t_mu,), inc="c")
        t_bc = b.emit("t", lambda e: e.matmul(psF[:, :], ones[0:1, 0:64],
                                              fac_r[0:1, 16:32], start=True, stop=True),
                      waits=(t_ex,), inc="c")
        t_f6 = b.emit("v", lambda e: e.tensor_copy(fac64[:], psF[:, :]),
                      waits=(t_bc,), inc="c")
        s_fac.prod = t_f6

        def fwo(e):
            for ec in range(NDC):
                inst = e.dma_start(out=woH[:, ec * 1024:(ec + 1) * 1024],
                                   in_=dt["wo1T"][ec * 128:(ec + 1) * 128, :])
            return inst
        c.produce(s_wo, "y", fwo, inc="d")

        def loadkv(tt):
            i = tt % 2
            c.produce(s_kt[i], "y", lambda e, tt=tt, i=i: e.dma_start(
                out=ktH[:, i * 1024:(i + 1) * 1024],
                in_=dt["k2_s"][tt * 128:(tt + 1) * 128, :]), inc="d")
            c.produce(s_vt[i], "y", lambda e, tt=tt, i=i: e.dma_start(
                out=vtH[:, i * 1024:(i + 1) * 1024],
                in_=dt["v2_s"][tt * 128:(tt + 1) * 128, :]), inc="d")

        loadkv(0)
        for tt in range(NT):
            i = tt % 2
            if tt + 1 < NT:
                loadkv(tt + 1)
            def fkv(e, tt=tt, i=i):
                for h in range(16):
                    inst = e.matmul(
                        psKV[:, h * 64:(h + 1) * 64],
                        ktH[:, i * 1024 + h * 64: i * 1024 + (h + 1) * 64],
                        vtH[:, i * 1024 + h * 64: i * 1024 + (h + 1) * 64],
                        start=(tt == 0 and h % 8 == 0), stop=(tt == NT - 1))
                return inst
            c.xfer([s_kt[i], s_vt[i]], s_kv, "t", fkv,
                   first=(tt == 0), last=(tt == NT - 1))
        def fkve(e):
            for h in range(16):
                inst = e.activation(
                    kvH[(h % 2) * 64:(h % 2) * 64 + 64,
                        (h // 2) * 64:(h // 2 + 1) * 64],
                    psKV[:, h * 64:(h + 1) * 64], AF.Copy,
                    scale=fac64[0:64, h:h + 1])
            return inst
        c.xfer([s_kv, s_fac], s_kvsb, "s", fkve)

        def loadqn(ts):
            i = ts % 2
            def fq(e, ts=ts, i=i):
                for ec in range(8):
                    inst = e.dma_start(
                        out=qnH[:, i * 4096 + ec * 512: i * 4096 + (ec + 1) * 512],
                        in_=dt["q2T_s"][ec * 128:(ec + 1) * 128, ts * 512:(ts + 1) * 512])
                return inst
            c.produce(s_qn[i], "y", fq, inc="d")
            for tt in range(4):
                c.produce(s_xH[i][tt], "y", lambda e, ts=ts, tt=tt, i=i: e.dma_start(
                    out=s_xH[i][tt].ap,
                    in_=dt["x_s"][ts * 512 + tt * 128:ts * 512 + (tt + 1) * 128, :]),
                    inc="d")

        loadqn(0)
        for ts in range(NTS):
            i = ts % 2
            if ts + 1 < NTS:
                loadqn(ts + 1)
            for h2 in range(8):
                def fat(e, h2=h2, i=i):
                    e.matmul(psAT[0][:, :],
                             kvH[0:64, h2 * 64:(h2 + 1) * 64],
                             qnH[0:64, i * 4096 + h2 * 512: i * 4096 + (h2 + 1) * 512],
                             start=True, stop=True, tile_position=(0, 0))
                    return e.matmul(psAT[1][:, :],
                                    kvH[64:128, h2 * 64:(h2 + 1) * 64],
                                    qnH[64:128, i * 4096 + h2 * 512:
                                        i * 4096 + (h2 + 1) * 512],
                                    start=True, stop=True, tile_position=(64, 0))
                w = [s_kvsb.prod, s_qn[i].prod] + s_psAT[0].cons + s_psAT[1].cons + \
                    [s_psAT[0].prod, s_psAT[1].prod]
                tk = b.emit("t", fat, waits=w, inc="c")
                s_kvsb.cons.append(tk); s_qn[i].cons.append(tk)
                s_psAT[0].prod = tk; s_psAT[0].cons = []
                s_psAT[1].prod = tk; s_psAT[1].cons = []
                def fev(e, h2=h2):
                    e.tensor_copy(atH[0:64, h2 * 512:(h2 + 1) * 512], psAT[0][:, :])
                    return e.tensor_copy(atH[64:128, h2 * 512:(h2 + 1) * 512],
                                         psAT[1][:, :])
                c.xfer([s_psAT[0], s_psAT[1]], s_at[h2], "v", fev)
            if DEV.get("H_dbg"):
                def fdbg(e, ts=ts):
                    for ec in range(8):
                        inst = e.dma_start(
                            out=dt["kT_s"][ec * 128:(ec + 1) * 128,
                                           ts * 512:(ts + 1) * 512],
                            in_=atH[:, ec * 512:(ec + 1) * 512])
                    return inst
                c.consume(list(s_at), "g", fdbg, inc="d")
            for oc in range(8):
                def fo(e, oc=oc):
                    for ec in range(8):
                        inst = e.matmul(
                            psP[oc % 2][:],
                            woH[:, ec * 1024 + oc * 128: ec * 1024 + (oc + 1) * 128],
                            atH[:, ec * 512:(ec + 1) * 512],
                            start=(ec == 0), stop=(ec == 7))
                    return inst
                c.xfer(list(s_at) + [s_wo], s_psP[oc % 2], "t", fo)
                c.xfer([s_psP[oc % 2]], s_aT[oc], "v",
                       lambda e, oc=oc: e.tensor_copy(aTH[:, oc * 512:(oc + 1) * 512],
                                                      psP[oc % 2][:]))
            for tt in range(4):
                it = tt % 2
                for half in range(2):
                    def ftr(e, tt=tt, it=it, half=half):
                        for o4 in range(4):
                            oc = half * 4 + o4
                            inst = e.transpose(
                                psP[it][:, o4 * 128:(o4 + 1) * 128],
                                aTH[:, oc * 512 + tt * 128: oc * 512 + (tt + 1) * 128],
                                eye_sb[:])
                        return inst
                    c.xfer(list(s_aT), s_psP[it], "t", ftr)
                    c.xfer([s_psP[it]], s_atok[it], "v",
                           lambda e, it=it, half=half: e.tensor_copy(
                               atokH[:, it * 1024 + half * 512:
                                     it * 1024 + (half + 1) * 512],
                               psP[it][:]),
                           first=(half == 0), last=(half == 1))
                lnH.run(c, s_atok[it], s_lnH.ap, s_lnH)
                c.xfer([s_lnH, s_xH[i][tt]], s_x3[it], "v",
                       lambda e, it=it, i=i, tt=tt: e.tensor_add(
                           s_x3[it].ap, s_xH[i][tt].ap, lnoH[:]))
                c.consume([s_x3[it]], "s", lambda e, ts=ts, tt=tt, it=it: e.dma_start(
                    out=dt["x_s"][ts * 512 + tt * 128:ts * 512 + (tt + 1) * 128, :],
                    in_=s_x3[it].ap), inc="d")
        b.barrier()
        esH.close()
        if finish(9):
            return nc, gstack, dt

        esI = ExitStack()
        xI = esI.enter_context(nc.sbuf_tensor("xI", [128, 2 * 4 * 1024], F32))
        xnI = esI.enter_context(nc.sbuf_tensor("xnI", [128, 4 * 1024], F32))
        xnTI = esI.enter_context(nc.sbuf_tensor("xnTI", [128, 4096], F32))
        w1I = esI.enter_context(nc.sbuf_tensor("w1I", [128, 2 * 1024], F32))
        w2I = esI.enter_context(nc.sbuf_tensor("w2I", [128, 2 * 512], F32))
        hI = esI.enter_context(nc.sbuf_tensor("hI", [128, 32 * 512], F32))
        b2I = esI.enter_context(nc.sbuf_tensor("b2I", [1, 1024], F32))
        b1I = esI.enter_context(nc.sbuf_tensor("b1I", [128, 32], F32))
        oI = esI.enter_context(nc.sbuf_tensor("oI", [128, 4 * 1024], F32))
        lnoI = esI.enter_context(nc.sbuf_tensor("lnoI", [128, 1024], F32))
        yI = esI.enter_context(nc.sbuf_tensor("yI", [128, 2 * 1024], F32))
        lnI = LNKit(nc, esI, "I")
        psI = [esI.enter_context(nc.psum_tensor(f"psI{i}", [128, 512], F32))
               for i in range(2)]
        psJ = [esI.enter_context(nc.psum_tensor(f"psJ{i}", [128, 512], F32))
               for i in range(4)]

        s_xI = [[_attach(Slot(), xI[:, (i * 4 + tt) * 1024:(i * 4 + tt + 1) * 1024])
                 for tt in range(4)] for i in range(2)]
        s_xnI = [_attach(Slot(), xnI[:, tt * 1024:(tt + 1) * 1024]) for tt in range(4)]
        s_xnTI = [Slot() for _ in range(NDC)]
        s_w1 = slots(2)
        s_w2 = slots(2)
        s_h = slots(32)
        s_psI = slots(2)
        s_psJ = slots(4)
        s_o4 = [_attach(Slot(), oI[:, tt_ * 1024:(tt_ + 1) * 1024]) for tt_ in range(4)]
        s_lnI = _attach(Slot(), lnoI[:])
        s_yI = [_attach(Slot(), yI[:, i * 1024:(i + 1) * 1024]) for i in range(2)]
        s_bI = Slot()

        def fb(e):
            e.dma_start(out=b1I[:], in_=dt["b1fT"][:, :])
            return e.dma_start(out=b2I[:], in_=dt["b2f"][:, :])
        c.produce(s_bI, "y", fb, inc="d")

        def loadI(ts):
            i = ts % 2
            for tt in range(4):
                c.produce(s_xI[i][tt], "y", lambda e, ts=ts, tt=tt, i=i: e.dma_start(
                    out=s_xI[i][tt].ap,
                    in_=dt["x_s"][ts * 512 + tt * 128:ts * 512 + (tt + 1) * 128, :]),
                    inc="d")

        loadI(0)
        for ts in range(NTS):
            i = ts % 2
            if ts + 1 < NTS:
                loadI(ts + 1)
            for tt in range(4):
                lnI.run(c, s_xI[i][tt], s_xnI[tt].ap, s_xnI[tt])
            for dc in range(NDC):
                def ftr(e, dc=dc):
                    for tt in range(4):
                        inst = e.transpose(
                            psI[dc % 2][:, tt * 128:(tt + 1) * 128],
                            xnI[:, tt * 1024 + dc * 128: tt * 1024 + (dc + 1) * 128],
                            eye_sb[:])
                    return inst
                c.xfer(s_xnI, s_psI[dc % 2], "t", ftr)
                c.xfer([s_psI[dc % 2]], s_xnTI[dc], "v",
                       lambda e, dc=dc: e.tensor_copy(
                           xnTI[:, dc * 512:(dc + 1) * 512], psI[dc % 2][:]))
            for ic in range(32):
                iw = ic % 2
                def fw(e, ic=ic, iw=iw):
                    for dc in range(NDC):
                        inst = e.dma_start(
                            out=w1I[:, iw * 1024 + dc * 128: iw * 1024 + (dc + 1) * 128],
                            in_=dt["w1f"][dc * 128:(dc + 1) * 128, ic * 128:(ic + 1) * 128])
                    return inst
                c.produce(s_w1[iw], "y", fw, inc="d")
                def fm(e, ic=ic, iw=iw):
                    for dc in range(NDC):
                        inst = e.matmul(
                            psI[iw][:],
                            w1I[:, iw * 1024 + dc * 128: iw * 1024 + (dc + 1) * 128],
                            xnTI[:, dc * 512:(dc + 1) * 512],
                            start=(dc == 0), stop=(dc == NDC - 1))
                    return inst
                c.xfer(s_xnTI + [s_w1[iw]], s_psI[iw], "t", fm)
                c.xfer([s_psI[iw], s_bI], s_h[ic], "v",
                       lambda e, ic=ic, iw=iw: e.tensor_scalar(
                           hI[:, ic * 512:(ic + 1) * 512], psI[iw][:],
                           b1I[:, ic:ic + 1], 0.0, ALU.add, ALU.max))
            for dc2 in range(2):
                for ic in range(32):
                    iw2 = ic % 2
                    c.produce(s_w2[iw2], "y", lambda e, ic=ic, dc2=dc2, iw2=iw2:
                              e.dma_start(out=w2I[:, iw2 * 512:(iw2 + 1) * 512],
                                          in_=dt["w2f"][ic * 128:(ic + 1) * 128,
                                                        dc2 * 512:(dc2 + 1) * 512]),
                              inc="d")
                    def fm2(e, ic=ic, iw2=iw2):
                        for tt in range(4):
                            inst = e.matmul(
                                psJ[tt][:],
                                hI[:, ic * 512 + tt * 128: ic * 512 + (tt + 1) * 128],
                                w2I[:, iw2 * 512:(iw2 + 1) * 512],
                                start=(ic == 0), stop=False)
                        return inst
                    wl = [s_h[ic].prod, s_w2[iw2].prod]
                    if ic == 0:
                        for tt in range(4):
                            wl += s_psJ[tt].cons + [s_psJ[tt].prod]
                    tk = b.emit("t", fm2, waits=wl, inc="c")
                    s_h[ic].cons.append(tk)
                    s_w2[iw2].cons.append(tk)
                    if ic == 0:
                        for tt in range(4):
                            s_psJ[tt].cons = []
                def fbias(e, dc2=dc2):
                    for tt in range(4):
                        inst = e.matmul(psJ[tt][:], ones[0:1, 0:128],
                                        b2I[:, dc2 * 512:(dc2 + 1) * 512],
                                        start=False, stop=True)
                    return inst
                tkb = b.emit("t", fbias, waits=[s_bI.prod], inc="c")
                s_bI.cons.append(tkb)
                for tt in range(4):
                    s_psJ[tt].prod = tkb
                    s_psJ[tt].cons = []
                for tt in range(4):
                    c.xfer([s_psJ[tt]], s_o4[tt], "v",
                           lambda e, tt=tt, dc2=dc2: e.tensor_copy(
                               oI[:, tt * 1024 + dc2 * 512: tt * 1024 + (dc2 + 1) * 512],
                               psJ[tt][:]),
                           first=(dc2 == 0), last=(dc2 == 1))
            for tt in range(4):
                it = tt % 2
                lnI.run(c, s_o4[tt], s_lnI.ap, s_lnI)
                c.xfer([s_lnI, s_xI[i][tt]], s_yI[it], "v",
                       lambda e, it=it, i=i, tt=tt: e.tensor_add(
                           s_yI[it].ap, s_xI[i][tt].ap, lnoI[:]))
                c.consume([s_yI[it]], "s", lambda e, ts=ts, tt=tt, it=it: e.dma_start(
                    out=y[ts * 512 + tt * 128:ts * 512 + (tt + 1) * 128, :],
                    in_=s_yI[it].ap), inc="d")
        b.barrier()
        esI.close()

    return nc, gstack, dt



def _prep(inputs):
    f = np.float32
    pre = {}
    pre["wqkv"] = np.ascontiguousarray(inputs["L0_attn_w_qkv"], dtype=f)
    pre["wout"] = np.ascontiguousarray(inputs["L0_attn_w_out"], dtype=f)
    pre["r_w"] = np.ascontiguousarray(inputs["L0_ffn_router_w"], dtype=f)
    pre["r_b"] = np.ascontiguousarray(np.asarray(inputs["L0_ffn_router_b"]).reshape(1, 64), dtype=f)
    pre["w1"] = np.ascontiguousarray(inputs["L0_ffn_w1"], dtype=f)
    pre["b1T"] = np.ascontiguousarray(
        np.asarray(inputs["L0_ffn_b1"]).reshape(64, 2, 128).transpose(2, 1, 0).reshape(128, 128),
        dtype=f)
    pre["w2"] = np.ascontiguousarray(inputs["L0_ffn_w2"], dtype=f)
    pre["b2"] = np.ascontiguousarray(inputs["L0_ffn_b2"], dtype=f)
    pre["sw1"] = np.ascontiguousarray(inputs["L0_ffn_sw1"], dtype=f)
    pre["sb1T"] = np.ascontiguousarray(
        np.asarray(inputs["L0_ffn_sb1"]).reshape(2, 16, 128).transpose(2, 0, 1).reshape(128, 32),
        dtype=f)
    pre["sw2"] = np.ascontiguousarray(inputs["L0_ffn_sw2"], dtype=f)
    pre["sb2r"] = np.ascontiguousarray(np.asarray(inputs["L0_ffn_sb2"]).reshape(1, 2048), dtype=f)
    pre["wq1T"] = np.ascontiguousarray(np.asarray(inputs["L1_attn_Wq"]).T, dtype=f)
    pre["wk1T"] = np.ascontiguousarray(np.asarray(inputs["L1_attn_Wk"]).T, dtype=f)
    pre["wv1T"] = np.ascontiguousarray(np.asarray(inputs["L1_attn_Wv"]).T, dtype=f)
    pre["wo1T"] = np.ascontiguousarray(np.asarray(inputs["L1_attn_Wo"]).T, dtype=f)
    pre["ncst"] = np.ascontiguousarray(np.asarray(inputs["L1_attn_norm_const"]).reshape(1, 16), dtype=f)
    pre["w1f"] = np.ascontiguousarray(inputs["L1_ffn_W1"], dtype=f)
    pre["b1fT"] = np.ascontiguousarray(np.asarray(inputs["L1_ffn_b1"]).reshape(32, 128).T, dtype=f)
    pre["w2f"] = np.ascontiguousarray(inputs["L1_ffn_W2"], dtype=f)
    pre["b2f"] = np.ascontiguousarray(np.asarray(inputs["L1_ffn_b2"]).reshape(1, D), dtype=f)
    pre["eye"] = np.eye(128, dtype=f)
    return pre


def kernel(**inputs):
    pre = _prep(inputs)
    nc, gstack, _ = build()
    x = np.ascontiguousarray(np.asarray(inputs["L0_attn_x"]), dtype=np.float32)
    in_maps = []
    for b_i in range(NCORE):
        m = {"x_in": np.ascontiguousarray(x[b_i])}
        m.update(pre)
        in_maps.append(m)
    res = run_bass_kernel_spmd(nc, in_maps, core_ids=list(range(NCORE)))
    gstack.close()
    return np.stack([r["y"] for r in res.results], axis=0)


# version 10 (final)
# speedup vs baseline: 1.0714x; 1.0714x over previous
import math
from contextlib import ExitStack

import numpy as np
import concourse.bass as bass
import concourse.mybir as mybir
from concourse.bass import broadcast_tensor_aps
from concourse.bass_utils import run_bass_kernel_spmd

F32 = mybir.dt.float32
BF16 = mybir.dt.bfloat16
AF = mybir.ActivationFunctionType
ALU = mybir.AluOpType
AX = mybir.AxisListType

T, D = 4096, 1024
NH, DH = 16, 64
NCORE = 8
NT = T // 128
NTS = T // 512
NDC = D // 128
EPS = 1e-5
DEV = {}



class Bld:

    ENG = {"t": "tensor", "v": "vector", "s": "scalar", "g": "gpsimd", "y": "sync"}

    def __init__(self, nc, block):
        self.nc, self.block = nc, block
        self.phase = 0
        self.sems = {}
        self.cnt = {}
        self.lastw = {}
        self.grp_map = {}
        self.grp_next = 0
        self.phase_keys = set()

    def _sem(self, key):
        if key not in self.sems:
            self.sems[key] = self.nc.alloc_semaphore(
                name="sm_%s_%s_%s" % (key[0], key[1], key[2]))
            self.cnt[key] = 0
        return self.sems[key]

    def emit(self, eng, fn, waits=(), inc="c", grp=None):
        if inc == "d":
            if grp == "sh":
                gi = "sh"
            else:
                gi = self.grp_map.get(grp)
                if gi is None:
                    gi = self.grp_next
                    self.grp_map[grp] = gi
                    self.grp_next += 1
            key = (eng, "d", gi)
        else:
            key = (eng, "c", self.phase % 3)
        if inc:
            sem = self._sem(key)
            step = 16 if inc == "d" else 1
            self.phase_keys.add(key)
        out = []

        def body(e):
            for tk in waits:
                if tk is None:
                    continue
                k, v = tk
                if v <= 0:
                    continue
                if self.lastw.get((eng, k), 0) < v:
                    e.wait_ge(self.sems[k], v)
                    self.lastw[(eng, k)] = v
            if inc == "d":
                fn(_DmaWrap(e, self, sem, key))
                out.append((key, self.cnt[key]))
            else:
                inst = fn(e)
                if inc:
                    self.cnt[key] += step
                    inst.then_inc(sem, step)
                    out.append((key, self.cnt[key]))

        getattr(self.block, self.ENG[eng])(body)
        return out[0] if out else None

    def dma(self, eng, out, in_, waits=(), grp=None):
        return self.emit(eng, lambda e: e.dma_start(out=out, in_=in_), waits,
                         inc="d", grp=grp)

    def tok_all(self, eng, kind="c", grp=None):
        if kind == "d":
            key = (eng, "d", self.grp_map.get(grp, -1))
        else:
            key = (eng, "c", self.phase % 3)
        return (key, self.cnt.get(key, 0))

    def barrier(self):
        finals = [(k, self.cnt[k]) for k in sorted(self.phase_keys, key=str)
                  if self.cnt.get(k, 0) > 0]
        for eng in ("t", "v", "s", "g", "y"):
            def body(e, fins=tuple(finals), eng=eng):
                for k, v in fins:
                    if self.lastw.get((eng, k), 0) < v:
                        e.wait_ge(self.sems[k], v)
                        self.lastw[(eng, k)] = v
            getattr(self.block, self.ENG[eng])(body)
        self.phase += 1
        self.phase_keys = set()
        self.grp_map = {}
        self.grp_next = 0


class _DmaWrap:

    def __init__(self, e, bld, sem, key):
        self._e, self._bld, self._sem, self._key = e, bld, sem, key

    def dma_start(self, **kw):
        inst = self._e.dma_start(**kw)
        self._bld.cnt[self._key] += 16
        inst.then_inc(self._sem, 16)
        return inst

    def __getattr__(self, n):
        return getattr(self._e, n)


class Slot:
    __slots__ = ("prod", "cons", "ap")

    def __init__(self):
        self.prod = None
        self.cons = []
        self.ap = None


def slots(n):
    return [Slot() for _ in range(n)]


class Ctx:
    def __init__(self, b):
        self.b = b

    def produce(self, dst, eng, fn, waits=(), inc="c", first=True, last=True):
        b = self.b
        w = list(waits)
        if first:
            w += dst.cons + [dst.prod]
        tok = b.emit(eng, fn, waits=w, inc=inc if last else None, grp=id(dst))
        if last:
            dst.prod = tok
            dst.cons = []
        return tok

    def consume(self, srcs, eng, fn, waits=(), inc="c"):
        if isinstance(srcs, Slot):
            srcs = [srcs]
        w = [s.prod for s in srcs] + list(waits)
        tok = self.b.emit(eng, fn, waits=w, inc=inc, grp=id(srcs[0]))
        if tok:
            for s in srcs:
                s.cons.append(tok)
        return tok

    def xfer(self, srcs, dst, eng, fn, waits=(), inc="c", first=True, last=True):
        if isinstance(srcs, Slot):
            srcs = [srcs]
        b = self.b
        w = [s.prod for s in srcs] + list(waits)
        if first:
            w += dst.cons + [dst.prod]
        tok = b.emit(eng, fn, waits=w, inc=inc, grp=id(dst))
        if tok:
            for s in srcs:
                s.cons.append(tok)
        if last:
            dst.prod = tok
            dst.cons = []
        return tok


class LNKit:

    def __init__(self, nc, es, tag):
        self.trash = es.enter_context(nc.sbuf_tensor(f"ln_tr_{tag}", [128, D], F32))
        self.colbuf = es.enter_context(nc.sbuf_tensor(f"ln_c_{tag}", [128, 24], F32))
        self.cslots = slots(2)
        self.idx = 0

    def run(self, c, xs, out, outs, extra_waits=()):
        b = c.b
        i = self.idx % 2
        self.idx += 1
        cs = self.cslots[i]
        col = self.colbuf[:, i * 12:i * 12 + 12]
        (ss, sm, mean, em2, m2, var, rv, rstd,
         nmr, negm) = (col[:, j:j + 1] for j in range(10))
        x_ap = xs.ap

        t1a = c.xfer([xs], cs, "v",
                     lambda e: e.tensor_mul(self.trash[:], x_ap, x_ap),
                     waits=extra_waits, inc="c")
        def fsums(e):
            e.reduce_sum(ss, self.trash[:], axis=AX.X)
            return e.reduce_sum(sm, x_ap, axis=AX.X)
        t2 = c.xfer([xs], cs, "v", fsums, waits=(t1a,), inc="c", first=False)

        def f2(e):
            e.tensor_scalar_mul(mean, sm, 1.0 / D)
            e.tensor_scalar_mul(em2, ss, 1.0 / D)
            e.tensor_mul(m2, mean, mean)
            e.tensor_sub(var, em2, m2)
            e.tensor_scalar_add(var, var, EPS)
            return e.tensor_scalar_mul(negm, mean, -1.0)
        t3 = c.produce(cs, "g", f2, waits=(t2,), inc="c", first=False)
        t3b = c.produce(cs, "v", lambda e: e.reciprocal(rv, var),
                        waits=(t3,), first=False)
        t4 = c.produce(cs, "s",
                       lambda e: e.activation(rstd, rv, AF.Sqrt),
                       waits=(t3b,), first=False)
        t5 = c.produce(cs, "g", lambda e: e.tensor_mul(nmr, negm, rstd),
                       waits=(t4,), first=False)
        t6 = c.xfer([xs, cs], outs, "v",
                    lambda e: e.tensor_scalar_mul(out, x_ap, rstd),
                    waits=(t4,), inc="c")
        oo, nn = broadcast_tensor_aps(out, nmr)
        t7 = c.xfer([cs], outs, "v",
                    lambda e: e.tensor_add(oo, oo, nn),
                    waits=(t5, t6), first=False, last=True)
        return t7


class SB:

    def __init__(self, nc, es, name, cols, nslots=1):
        self.h = es.enter_context(nc.sbuf_tensor(name, [128, cols], F32))
        self.sl = slots(nslots)

    def __getitem__(self, idx):
        return self.h[idx]


def _attach(slot, ap):
    slot.ap = ap
    return slot



def build(dump=None):
    nc = bass.Bass("TRN2", target_bir_lowering=False)

    dt = {}
    def din(name, shape):
        dt[name] = nc.dram_tensor(name, shape, F32, kind="ExternalInput").ap()
    def dscr(name, shape):
        dt[name] = nc.dram_tensor(name, shape, F32, kind="Internal").ap()

    din("x_in", [T, D])
    din("wqkv", [D, 3 * D])
    din("wout", [D, D])
    din("r_w", [D, 64])
    din("r_b", [1, 64])
    din("w1", [64, D, 256])
    din("b1T", [128, 128])
    din("w2", [64, 256, D])
    din("b2", [64, D])
    din("sw1", [2, D, 2048])
    din("sb1T", [128, 32])
    din("sw2", [2, 2048, D])
    din("sb2r", [1, 2048])
    din("wq1T", [D, D])
    din("wk1T", [D, D])
    din("wv1T", [D, D])
    din("wo1T", [D, D])
    din("ncst", [1, 16])
    din("w1f", [D, 4096])
    din("b1fT", [128, 32])
    din("w2f", [4096, D])
    din("b2f", [1, D])
    din("eye", [128, 128])

    dscr("x_s", [T, D])
    dt["xh_s"] = nc.dram_tensor("xh_s", [D, T], BF16, kind="Internal").ap()
    dt["xl_s"] = nc.dram_tensor("xl_s", [D, T], BF16, kind="Internal").ap()
    dscr("qT_s", [D, T])
    dscr("kT_s", [D, T])
    dscr("v_s", [T, D])
    dscr("OT_s", [D, T])
    dscr("xnT_s", [D, T])
    dscr("acc_s", [T, D])
    dscr("cw_s", [64, T])
    dscr("q2T_s", [D, T])
    dscr("k2_s", [T, D])
    dscr("v2_s", [T, D])

    y = nc.dram_tensor("y", [T, D], F32, kind="ExternalOutput").ap()
    dbg = None
    if dump is not None:
        shp = dict(x_s=[T, D], qT_s=[D, T], kT_s=[D, T], v_s=[T, D],
                   OT_s=[D, T], xnT_s=[D, T], acc_s=[T, D], cw_s=[64, T],
                   q2T_s=[D, T], k2_s=[T, D], v2_s=[T, D], y=[T, D])[dump[0]]
        dbg = nc.dram_tensor("dbg", shp, F32, kind="ExternalOutput").ap()

    gstack = ExitStack()
    done = [False]

    with nc.Block() as block:
        b = Bld(nc, block)
        c = Ctx(b)

        ones = gstack.enter_context(nc.sbuf_tensor("ones", [1, 128], F32))
        cpool = gstack.enter_context(nc.sbuf_tensor("cpool", [128, 2], F32))
        eye_sb = gstack.enter_context(nc.sbuf_tensor("eye_sb", [128, 128], F32))
        bias_sb = gstack.enter_context(nc.sbuf_tensor("bias_sb", [128, 16], F32))
        esMX = ExitStack()
        maxf = esMX.enter_context(nc.sbuf_tensor("maxf", [128, 128], F32))

        b.emit("v", lambda e: e.memset(ones[:], 1.0), inc=None)
        b.emit("v", lambda e: e.memset(cpool[:, 0:1], EPS), inc=None)
        b.emit("v", lambda e: e.memset(cpool[:, 1:2], 1.0 / 32.0), inc=None)
        b.eps_ap = cpool[:, 0:1]
        b.s32_ap = cpool[:, 1:2]
        b.emit("v", lambda e: e.memset(maxf[:], 0.0), inc="c")
        b.dma("y", eye_sb[:], dt["eye"][:, :])
        b.barrier()

        def finish(ph):
            if done[0] or dump is None or dump[1] != ph:
                return False
            src = dt[dump[0]] if dump[0] != "y" else y
            R, C = src.shape
            with nc.sbuf_tensor("dbg_buf", [128, C], F32) as bb:
                prev = None
                for r in range(0, R, 128):
                    n = min(128, R - r)
                    tl = b.dma("y", bb[0:n, :], src[r:r + n, :],
                               waits=(prev,))
                    prev = b.dma("g", dbg[r:r + n, :], bb[0:n, :], waits=(tl,))
                b.barrier()
            done[0] = True
            return True

        esA = ExitStack()
        wq_sb = esA.enter_context(nc.sbuf_tensor("wq_sb", [128, 8 * 3072], F32))
        x_sl = esA.enter_context(nc.sbuf_tensor("x_sl", [128, 4 * 1024], F32))
        xn_sl = esA.enter_context(nc.sbuf_tensor("xn_sl", [128, 4 * 1024], F32))
        xnT = esA.enter_context(nc.sbuf_tensor("xnT", [128, 4096], F32))
        ev = esA.enter_context(nc.sbuf_tensor("ev", [128, 4 * 512], F32))
        vT_b = esA.enter_context(nc.sbuf_tensor("vT_b", [128, 2 * 512], F32))
        v_sl = esA.enter_context(nc.sbuf_tensor("v_sl", [128, 4096], F32))
        lnA = LNKit(nc, esA, "A")
        ps_tr = [esA.enter_context(nc.psum_tensor(f"psA_tr{i}", [128, 512], F32))
                 for i in range(2)]
        ps_p = [esA.enter_context(nc.psum_tensor(f"psA_p{i}", [128, 512], F32))
                for i in range(4)]

        s_w = Slot()
        s_x = [_attach(Slot(), x_sl[:, tt * 1024:(tt + 1) * 1024]) for tt in range(4)]
        s_xn = [_attach(Slot(), xn_sl[:, tt * 1024:(tt + 1) * 1024]) for tt in range(4)]
        s_xnT = [Slot() for _ in range(NDC)]
        s_pstr = slots(2)
        s_psp = slots(4)
        s_ev = slots(4)
        s_vT = slots(2)
        s_vblk = [slots(8) for _ in range(4)]

        def wload(e):
            for dc in range(NDC):
                inst = e.dma_start(out=wq_sb[:, dc * 3072:(dc + 1) * 3072],
                                   in_=dt["wqkv"][dc * 128:(dc + 1) * 128, :])
            return inst
        c.produce(s_w, "y", wload, inc="d")

        def loadx(ts):
            for tt in range(4):
                c.produce(s_x[tt], "y", lambda e, ts=ts, tt=tt: e.dma_start(
                    out=s_x[tt].ap,
                    in_=dt["x_in"][ts * 512 + tt * 128: ts * 512 + (tt + 1) * 128, :]),
                    inc="d")

        for ts in range(NTS):
            loadx(ts)
            for tt in range(4):
                lnA.run(c, s_x[tt], s_xn[tt].ap, s_xn[tt])
            for dc in range(NDC):
                pt = s_pstr[dc % 2]
                def ftr(e, dc=dc):
                    for tt in range(4):
                        inst = e.transpose(
                            ps_tr[dc % 2][:, tt * 128:(tt + 1) * 128],
                            xn_sl[:, tt * 1024 + dc * 128: tt * 1024 + (dc + 1) * 128],
                            eye_sb[:])
                    return inst
                c.xfer(s_xn, pt, "t", ftr)
                c.xfer([pt], s_xnT[dc], "v",
                       lambda e, dc=dc: e.tensor_copy(
                           xnT[:, dc * 512:(dc + 1) * 512], ps_tr[dc % 2][:]))
            for ec in range(24):
                pp = s_psp[ec % 4]
                def fmm(e, ec=ec):
                    for dc in range(NDC):
                        inst = e.matmul(
                            ps_p[ec % 4][:],
                            wq_sb[:, dc * 3072 + ec * 128: dc * 3072 + (ec + 1) * 128],
                            xnT[:, dc * 512:(dc + 1) * 512],
                            start=(dc == 0), stop=(dc == NDC - 1))
                    return inst
                c.xfer(s_xnT + [s_w], pp, "t", fmm)
                if ec < 16:
                    eb = ev[:, (ec % 4) * 512:(ec % 4 + 1) * 512]
                    c.xfer([pp], s_ev[ec % 4], "v",
                           lambda e, eb=eb, ec=ec: e.tensor_copy(eb, ps_p[ec % 4][:]))
                    mcl = (0 if ec < 8 else 64) + (ec % 8) * 8 + ts
                    c.consume([s_ev[ec % 4]], "v",
                              lambda e, eb=eb, mcl=mcl: e.reduce_max(
                                  maxf[:, mcl:mcl + 1], eb, axis=AX.X,
                                  apply_absolute_value=True))
                    dst = dt["qT_s"] if ec < 8 else dt["kT_s"]
                    col = ec % 8
                    c.consume([s_ev[ec % 4]], "s", lambda e, dst=dst, col=col, ts=ts, eb=eb:
                              e.dma_start(out=dst[col * 128:(col + 1) * 128,
                                                  ts * 512:(ts + 1) * 512], in_=eb),
                              inc="d")
                else:
                    ebi = (ec - 16)
                    vb = vT_b[:, (ec % 2) * 512:(ec % 2 + 1) * 512]
                    c.xfer([pp], s_vT[ec % 2], "v",
                           lambda e, vb=vb, ec=ec: e.tensor_copy(vb, ps_p[ec % 4][:]))
                    for tt in range(4):
                        pt = s_pstr[tt % 2]
                        c.xfer([s_vT[ec % 2]], pt, "t",
                               lambda e, vb=vb, tt=tt, ebi=ebi: e.transpose(
                                   ps_tr[tt % 2][:, (ebi % 4) * 128:(ebi % 4 + 1) * 128],
                                   vb[:, tt * 128:(tt + 1) * 128], eye_sb[:]))
                        c.xfer([pt], s_vblk[tt][ebi], "v",
                               lambda e, tt=tt, ebi=ebi: e.tensor_copy(
                                   v_sl[:, tt * 1024 + ebi * 128:
                                        tt * 1024 + (ebi + 1) * 128],
                                   ps_tr[tt % 2][:, (ebi % 4) * 128:(ebi % 4 + 1) * 128]))
                    if ebi == 7:
                        for tt in range(4):
                            c.consume(s_vblk[tt], "s",
                                      lambda e, tt=tt, ts=ts: e.dma_start(
                                          out=dt["v_s"][ts * 512 + tt * 128:
                                                        ts * 512 + (tt + 1) * 128, :],
                                          in_=v_sl[:, tt * 1024:(tt + 1) * 1024]),
                                      inc="d")
        b.barrier()
        esA.close()
        if finish(1):
            return nc, gstack, dt

        esP = ExitStack()
        qmf = esP.enter_context(nc.sbuf_tensor("qmf", [128, 16], F32))
        mrow = esP.enter_context(nc.sbuf_tensor("mrow", [8, 256], F32))
        mcol = esP.enter_context(nc.sbuf_tensor("mcol", [8, 8], F32))
        brow = esP.enter_context(nc.sbuf_tensor("brow", [1, 16], F32))
        psN0 = esP.enter_context(nc.psum_tensor("psN0", [128, 256], F32))
        s_ps = Slot()
        s_mrow = Slot()
        s_mcol = Slot()
        s_brow = Slot()
        s_bias = Slot()

        s_qmf = Slot()
        def fqm(e):
            for ec in range(16):
                inst = e.reduce_max(qmf[:, ec:ec + 1],
                                    maxf[:, ec * 8:(ec + 1) * 8], axis=AX.X)
            return inst
        c.produce(s_qmf, "v", fqm)
        def ftr2(e):
            e.transpose(psN0[0:8, 0:128], qmf[:, 0:8], eye_sb[:])
            return e.transpose(psN0[0:8, 128:256], qmf[:, 8:16], eye_sb[:])
        c.xfer([s_qmf], s_ps, "t", ftr2)
        c.xfer([s_ps], s_mrow, "v",
               lambda e: e.tensor_copy(mrow[0:8, 0:256], psN0[0:8, 0:256]))

        def fb(e):
            e.reduce_max(mcol[0:8, 0:1], mrow[0:8, 0:64], axis=AX.X)
            e.reduce_max(mcol[0:8, 1:2], mrow[0:8, 64:128], axis=AX.X)
            e.reduce_max(mcol[0:8, 2:3], mrow[0:8, 128:192], axis=AX.X)
            return e.reduce_max(mcol[0:8, 3:4], mrow[0:8, 192:256], axis=AX.X)
        tb1 = c.xfer([s_mrow], s_mcol, "v", fb)
        def fb2(e):
            e.tensor_mul(mcol[0:8, 4:5], mcol[0:8, 0:1], mcol[0:8, 2:3])
            e.tensor_mul(mcol[0:8, 5:6], mcol[0:8, 1:2], mcol[0:8, 3:4])
            return e.tensor_scalar_mul(mcol[0:8, 4:6], mcol[0:8, 4:6], -2.0)
        c.produce(s_mcol, "g", fb2, waits=(tb1,), first=False)
        def ftr3(e):
            e.transpose(psN0[0:1, 0:8], mcol[0:8, 4:5], eye_sb[0:8, 0:8])
            return e.transpose(psN0[0:1, 8:16], mcol[0:8, 5:6], eye_sb[0:8, 0:8])
        c.xfer([s_mcol], s_ps, "t", ftr3)
        c.xfer([s_ps], s_brow, "v",
               lambda e: e.tensor_copy(brow[0:1, 0:16], psN0[0:1, 0:16]))
        c.xfer([s_brow], s_ps, "t",
               lambda e: e.matmul(psN0[0:128, 0:16], ones[0:1, 0:128],
                                  brow[0:1, 0:16], start=True, stop=True))
        c.xfer([s_ps], s_bias, "v",
               lambda e: e.tensor_copy(bias_sb[:], psN0[0:128, 0:16]))
        b.barrier()
        esP.close()
        esMX.close()

        esB = ExitStack()
        kT_p = esB.enter_context(nc.sbuf_tensor("kT_p", [128, 2 * 4096], F32))
        v_p = esB.enter_context(nc.sbuf_tensor("v_p", [128, 2 * 4096], F32))
        qT_p = esB.enter_context(nc.sbuf_tensor("qT_p", [128, 2 * 256], F32))
        P_sb = esB.enter_context(nc.sbuf_tensor("P_sb", [128, 4 * 4096], F32))
        acc8 = esB.enter_context(nc.sbuf_tensor("acc8", [128, 4 * 10], F32))
        rinv = esB.enter_context(nc.sbuf_tensor("rinvB", [128, 8], F32))
        PT_s = esB.enter_context(nc.sbuf_tensor("PT_s", [128, 2 * 2 * 256], F32))
        Oraw = esB.enter_context(nc.sbuf_tensor("Oraw", [128, 256], F32))
        Otok = esB.enter_context(nc.sbuf_tensor("Otok", [128, 256], F32))
        OTst = esB.enter_context(nc.sbuf_tensor("OTst", [128, 256], F32))
        psS = [esB.enter_context(nc.psum_tensor(f"psS{i}", [128, 512], F32))
               for i in range(3)]
        psPT = [esB.enter_context(nc.psum_tensor(f"psPT{i}", [128, 512], F32))
                for i in range(2)]
        psPT2 = [esB.enter_context(nc.psum_tensor(f"psPT2{i}", [128, 512], F32))
                 for i in range(2)]
        psO = esB.enter_context(nc.psum_tensor("psO", [128, 256], F32))

        def psPTb(j, par):
            return (psPT if par == 0 else psPT2)[j]

        s_kT = slots(2)
        s_vp = slots(2)
        s_qT = slots(2)
        s_P = slots(4)
        s_acc = slots(4)
        s_rin = slots(8)
        s_PT = [slots(2) for _ in range(2)]
        s_psS = slots(3)
        s_psPT = [slots(2) for _ in range(2)]
        s_psO = Slot()
        s_Oraw = Slot()
        s_Otk = slots(2)
        s_OTq = slots(2)

        def load_pair(p):
            i = p % 2
            c.produce(s_kT[i], "y", lambda e, p=p, i=i: e.dma_start(
                out=kT_p[:, i * 4096:(i + 1) * 4096],
                in_=dt["kT_s"][p * 128:(p + 1) * 128, :]), inc="d")
            def fv(e, p=p, i=i):
                for kc in range(32):
                    inst = e.dma_start(
                        out=v_p[:, i * 4096 + kc * 128: i * 4096 + (kc + 1) * 128],
                        in_=dt["v_s"][kc * 128:(kc + 1) * 128, p * 128:(p + 1) * 128])
                return inst
            c.produce(s_vp[i], "y", fv, inc="d")

        NBP = DEV.get("B_p", 8)
        NBQ = DEV.get("B_qs", 16)
        exp_tok = {}
        pend_onorm = [None]

        def load_q(p, qs):
            iq = qs % 2
            c.produce(s_qT[iq], "y", lambda e, p=p, qs=qs, iq=iq: e.dma_start(
                out=qT_p[:, iq * 256:(iq + 1) * 256],
                in_=dt["qT_s"][p * 128:(p + 1) * 128, qs * 256:(qs + 1) * 256]),
                inc="d")

        load_pair(0)
        load_q(0, 0)
        for p in range(NBP):
            i = p % 2
            if p + 1 < NBP:
                load_pair(p + 1)
            for qs in range(NBQ):
                iq = qs % 2
                if qs + 1 < NBQ:
                    load_q(p, qs + 1)
                elif p + 1 < NBP:
                    load_q(p + 1, 0)
                if pend_onorm[0] is not None:
                    pend_onorm[0]()
                    pend_onorm[0] = None
                def emit_pt(kc):
                    par = kc % 2
                    for j in range(2):
                        def ftp(e, kc=kc, j=j, par=par):
                            for qt in range(2):
                                inst = e.transpose(
                                    psPTb(j, par)[:, qt * 128:(qt + 1) * 128],
                                    P_sb[:, (qt * 2 + j) * 4096 + kc * 128:
                                         (qt * 2 + j) * 4096 + (kc + 1) * 128],
                                    eye_sb[:])
                            return inst
                        w = list(s_psPT[j][par].cons) + [s_psPT[j][par].prod,
                             exp_tok[(0, j, kc // 4)], exp_tok[(1, j, kc // 4)]]
                        tk = b.emit("t", ftp, waits=w, inc="c")
                        s_P[0 * 2 + j].cons.append(tk)
                        s_P[1 * 2 + j].cons.append(tk)
                        s_psPT[j][par].prod = tk
                        s_psPT[j][par].cons = []
                        c.xfer([s_psPT[j][par]], s_PT[j][par], "v",
                               lambda e, j=j, par=par: e.tensor_copy(
                                   PT_s[:, (j * 2 + par) * 256:(j * 2 + par + 1) * 256],
                                   psPTb(j, par)[:, 0:256]))

                def emit_pv(kc):
                    par = kc % 2
                    for j in range(2):
                        c.xfer([s_PT[j][par], s_vp[i]], s_psO, "t",
                               lambda e, kc=kc, j=j, par=par, i=i: e.matmul(
                                   psO[j * 64:(j + 1) * 64, :],
                                   v_p[:, i * 4096 + kc * 128 + j * 64:
                                       i * 4096 + kc * 128 + (j + 1) * 64],
                                   PT_s[:, (j * 2 + par) * 256:(j * 2 + par + 1) * 256],
                                   start=(kc == 0), stop=(kc == 31),
                                   tile_position=(0, j * 64)),
                               first=(kc == 0 and j == 0),
                               last=(kc == 31 and j == 1))

                for qt in range(2):
                    for kc8 in range(8):
                        for j in range(2):
                            sp = s_P[qt * 2 + j]
                            sa = s_acc[qt * 2 + j]
                            ips = (2 * kc8 + j) % 3
                            c.xfer([s_qT[iq], s_kT[i]], s_psS[ips], "t",
                                   lambda e, ips=ips, j=j, qt=qt, kc8=kc8, i=i, iq=iq:
                                   e.matmul(psS[ips][:],
                                            qT_p[j * 64:(j + 1) * 64,
                                                 iq * 256 + qt * 128: iq * 256 + (qt + 1) * 128],
                                            kT_p[j * 64:(j + 1) * 64,
                                                 i * 4096 + kc8 * 512: i * 4096 + (kc8 + 1) * 512],
                                            start=True, stop=True,
                                            tile_position=(j * 64, 0)))
                            wx = (list(sa.cons) + [sa.prod]) if kc8 == 0 else ()
                            tk = c.xfer([s_psS[ips]], sp, "s",
                                        lambda e, ips=ips, qt=qt, j=j, kc8=kc8, p=p:
                                        e.activation(
                                            P_sb[:, (qt * 2 + j) * 4096 + kc8 * 512:
                                                 (qt * 2 + j) * 4096 + (kc8 + 1) * 512],
                                            psS[ips][:], AF.Exp, scale=b.s32_ap,
                                            bias=bias_sb[:, j * 8 + p:j * 8 + p + 1],
                                            accum_out=acc8[:, (qt * 2 + j) * 10 + kc8:
                                                           (qt * 2 + j) * 10 + kc8 + 1]),
                                        waits=wx, first=(kc8 == 0), last=(kc8 == 7))
                            exp_tok[(qt, j, kc8)] = tk
                        if qt == 1 and kc8 >= 1:
                            for kc in range(4 * (kc8 - 1), 4 * kc8):
                                emit_pt(kc)
                                if kc >= 1:
                                    emit_pv(kc - 1)
                    if qt == 1:
                        for kc in range(28, 32):
                            emit_pt(kc)
                            emit_pv(kc - 1)
                        emit_pv(31)
                    for j2 in range(2):
                        sp = s_P[qt * 2 + j2]
                        sa = s_acc[qt * 2 + j2]
                        sa.prod = sp.prod
                        sa.cons = []
                        g9 = (qt * 2 + j2) * 10
                        tsu = c.consume([sa], "v", lambda e, g9=g9: e.reduce_sum(
                            acc8[:, g9 + 8:g9 + 9], acc8[:, g9:g9 + 8], axis=AX.X))
                        tsc = c.consume([sa], "g", lambda e, g9=g9: e.tensor_copy(
                            acc8[:, g9 + 9:g9 + 10], acc8[:, g9 + 8:g9 + 9]),
                            waits=(tsu,))
                        rc = iq * 4 + qt * 2 + j2
                        c.xfer([sa], s_rin[rc], "v",
                               lambda e, rc=rc, g9=g9: e.reciprocal(
                                   rinv[:, rc:rc + 1],
                                   acc8[:, g9 + 9:g9 + 10]), waits=(tsc,))
                def onorm(p=p, qs=qs, iq=iq):
                    c.xfer([s_psO], s_Oraw, "v",
                           lambda e: e.tensor_copy(Oraw[:], psO[:]))
                    for qt in range(2):
                        c.xfer([s_Oraw], s_psPT[0][qt], "t",
                               lambda e, qt=qt: e.transpose(
                                   psPTb(0, qt)[:, 0:128],
                                   Oraw[:, qt * 128:(qt + 1) * 128], eye_sb[:]))
                        def fsc(e, qt=qt, iq=iq):
                            for j in range(2):
                                rc = iq * 4 + qt * 2 + j
                                inst = e.tensor_scalar_mul(
                                    Otok[:, qt * 128 + j * 64: qt * 128 + (j + 1) * 64],
                                    psPTb(0, qt)[:, j * 64:(j + 1) * 64],
                                    rinv[:, rc:rc + 1])
                            return inst
                        c.xfer([s_psPT[0][qt], s_rin[iq * 4 + qt * 2],
                                s_rin[iq * 4 + qt * 2 + 1]],
                               s_Otk[qt], "v", fsc)
                        c.xfer([s_Otk[qt]], s_psPT[1][qt], "t",
                               lambda e, qt=qt: e.transpose(
                                   psPTb(1, qt)[:, 0:128],
                                   Otok[:, qt * 128:(qt + 1) * 128], eye_sb[:]))
                        c.xfer([s_psPT[1][qt]], s_OTq[qt], "v",
                               lambda e, qt=qt: e.tensor_copy(
                                   OTst[:, qt * 128:(qt + 1) * 128],
                                   psPTb(1, qt)[:, 0:128]))
                    c.consume([s_OTq[0], s_OTq[1]], "g", lambda e, p=p, qs=qs:
                              e.dma_start(
                                  out=dt["OT_s"][p * 128:(p + 1) * 128,
                                                 qs * 256:(qs + 1) * 256],
                                  in_=OTst[:]), inc="d")
                pend_onorm[0] = onorm
        if pend_onorm[0] is not None:
            pend_onorm[0]()
            pend_onorm[0] = None
        b.barrier()
        esB.close()
        if finish(3):
            return nc, gstack, dt

        esC = ExitStack()
        wo_sb = esC.enter_context(nc.sbuf_tensor("wo_sb", [128, 8 * 1024], F32))
        ot_sl = esC.enter_context(nc.sbuf_tensor("ot_sl", [128, 2 * 4096], F32))
        at_sl = esC.enter_context(nc.sbuf_tensor("at_sl", [128, 4096], F32))
        a_tok = esC.enter_context(nc.sbuf_tensor("a_tok", [128, 2 * 1024], F32))
        x0_sl = esC.enter_context(nc.sbuf_tensor("x0_sl", [128, 2 * 4 * 1024], F32))
        lnout = esC.enter_context(nc.sbuf_tensor("lnoutC", [128, 1024], F32))
        xnew = esC.enter_context(nc.sbuf_tensor("xnewC", [128, 2 * 1024], F32))
        lnC = LNKit(nc, esC, "C")
        psA = [esC.enter_context(nc.psum_tensor(f"psC{i}", [128, 512], F32))
               for i in range(2)]
        psT = [esC.enter_context(nc.psum_tensor(f"psCT{i}", [128, 512], F32))
               for i in range(2)]

        s_wo = Slot()
        s_ot = slots(2)
        s_at = slots(8)
        s_atk = [_attach(Slot(), a_tok[:, i * 1024:(i + 1) * 1024]) for i in range(2)]
        s_x0 = [[_attach(Slot(), x0_sl[:, (i * 4 + tt) * 1024:(i * 4 + tt + 1) * 1024])
                 for tt in range(4)] for i in range(2)]
        s_ln = _attach(Slot(), lnout[:])
        s_xn2 = [_attach(Slot(), xnew[:, i * 1024:(i + 1) * 1024]) for i in range(2)]
        s_psA = slots(2)
        s_psT = slots(2)

        def fwo(e):
            for dc in range(NDC):
                inst = e.dma_start(out=wo_sb[:, dc * 1024:(dc + 1) * 1024],
                                   in_=dt["wout"][dc * 128:(dc + 1) * 128, :])
            return inst
        c.produce(s_wo, "y", fwo, inc="d")

        def loadC(ts):
            i = ts % 2
            def fot(e, ts=ts, i=i):
                for dc in range(NDC):
                    inst = e.dma_start(
                        out=ot_sl[:, i * 4096 + dc * 512: i * 4096 + (dc + 1) * 512],
                        in_=dt["OT_s"][dc * 128:(dc + 1) * 128, ts * 512:(ts + 1) * 512])
                return inst
            c.produce(s_ot[i], "y", fot, inc="d")
            for tt in range(4):
                c.produce(s_x0[i][tt], "y", lambda e, ts=ts, tt=tt, i=i: e.dma_start(
                    out=s_x0[i][tt].ap,
                    in_=dt["x_in"][ts * 512 + tt * 128:ts * 512 + (tt + 1) * 128, :]),
                    inc="d")

        loadC(0)
        for ts in range(NTS):
            i = ts % 2
            if ts + 1 < NTS:
                loadC(ts + 1)
            for oc in range(8):
                def fmm(e, oc=oc, i=i):
                    for dc in range(NDC):
                        inst = e.matmul(
                            psA[oc % 2][:],
                            wo_sb[:, dc * 1024 + oc * 128: dc * 1024 + (oc + 1) * 128],
                            ot_sl[:, i * 4096 + dc * 512: i * 4096 + (dc + 1) * 512],
                            start=(dc == 0), stop=(dc == NDC - 1))
                    return inst
                c.xfer([s_ot[i], s_wo], s_psA[oc % 2], "t", fmm)
                c.xfer([s_psA[oc % 2]], s_at[oc], "v",
                       lambda e, oc=oc: e.tensor_copy(
                           at_sl[:, oc * 512:(oc + 1) * 512], psA[oc % 2][:]))
            for tt in range(4):
                it = tt % 2
                for half in range(2):
                    def ftr2(e, tt=tt, it=it, half=half):
                        for o4 in range(4):
                            oc = half * 4 + o4
                            inst = e.transpose(
                                psT[it][:, o4 * 128:(o4 + 1) * 128],
                                at_sl[:, oc * 512 + tt * 128: oc * 512 + (tt + 1) * 128],
                                eye_sb[:])
                        return inst
                    c.xfer(s_at, s_psT[it], "t", ftr2, first=True, last=True)
                    c.xfer([s_psT[it]], s_atk[it], "v",
                           lambda e, it=it, half=half: e.tensor_copy(
                               a_tok[:, it * 1024 + half * 512: it * 1024 + (half + 1) * 512],
                               psT[it][:]),
                           first=(half == 0), last=(half == 1))
                lnC.run(c, s_atk[it], s_ln.ap, s_ln)
                c.xfer([s_ln, s_x0[i][tt]], s_xn2[it], "v",
                       lambda e, it=it, i=i, tt=tt: e.tensor_add(
                           s_xn2[it].ap, s_x0[i][tt].ap, lnout[:]))
                c.consume([s_xn2[it]], "s", lambda e, ts=ts, tt=tt, it=it: e.dma_start(
                    out=dt["x_s"][ts * 512 + tt * 128:ts * 512 + (tt + 1) * 128, :],
                    in_=s_xn2[it].ap), inc="d")
        b.barrier()
        esC.close()
        if finish(4):
            return nc, gstack, dt

        esD = ExitStack()
        xnTf = esD.enter_context(nc.sbuf_tensor("xnTf", [128, 32 * 1024], F32))
        xhD = esD.enter_context(nc.sbuf_tensor("xhD", [128, 2 * 1024], BF16))
        xlD = esD.enter_context(nc.sbuf_tensor("xlD", [128, 2 * 1024], BF16))
        xD_sl = esD.enter_context(nc.sbuf_tensor("xD_sl", [128, 2 * 1024], F32))
        xnD = esD.enter_context(nc.sbuf_tensor("xnD", [128, 2 * 1024], F32))
        rw_sb = esD.enter_context(nc.sbuf_tensor("rw_sb", [128, 8 * 64], F32))
        rb_sb = esD.enter_context(nc.sbuf_tensor("rb_sb", [1, 64], F32))
        rx = esD.enter_context(nc.sbuf_tensor("rxD", [128, 2 * 64], F32))
        rcol = esD.enter_context(nc.sbuf_tensor("rcolD", [128, 2 * 16], F32))
        mkD = esD.enter_context(nc.sbuf_tensor("mkD", [128, 2 * 128], F32))
        cwst = esD.enter_context(nc.sbuf_tensor("cwstD", [64, 2 * 128], F32))
        lnD = LNKit(nc, esD, "D")
        psD = [esD.enter_context(nc.psum_tensor(f"psD{i}", [128, 512], F32))
               for i in range(2)]
        psR = [esD.enter_context(nc.psum_tensor(f"psR{i}", [128, 64], F32))
               for i in range(2)]

        s_xD = [_attach(Slot(), xD_sl[:, i * 1024:(i + 1) * 1024]) for i in range(2)]
        s_xnD = [_attach(Slot(), xnD[:, i * 1024:(i + 1) * 1024]) for i in range(2)]
        s_xnTf = [Slot() for _ in range(NT)]
        s_xhD = slots(2)
        s_psD = slots(2)
        s_rw = Slot()
        s_psR = slots(2)
        s_rx = slots(2)
        s_rcol = slots(2)
        s_mk = slots(2)
        s_cwst = slots(2)

        def frw(e):
            for dc in range(NDC):
                e.dma_start(out=rw_sb[:, dc * 64:(dc + 1) * 64],
                            in_=dt["r_w"][dc * 128:(dc + 1) * 128, :])
            return e.dma_start(out=rb_sb[:], in_=dt["r_b"][:, :])
        c.produce(s_rw, "y", frw, inc="d")

        def loadD(tt):
            i = tt % 2
            c.produce(s_xD[i], "y", lambda e, tt=tt, i=i: e.dma_start(
                out=s_xD[i].ap, in_=dt["x_s"][tt * 128:(tt + 1) * 128, :]), inc="d")

        loadD(0)
        for tt in range(NT):
            i = tt % 2
            if tt + 1 < NT:
                loadD(tt + 1)
            lnD.run(c, s_xD[i], s_xnD[i].ap, s_xnD[i])
            for half in range(2):
                def ftr(e, tt=tt, i=i, half=half):
                    for d4 in range(4):
                        dc = half * 4 + d4
                        inst = e.transpose(
                            psD[i][:, d4 * 128:(d4 + 1) * 128],
                            xnD[:, i * 1024 + dc * 128: i * 1024 + (dc + 1) * 128],
                            eye_sb[:])
                    return inst
                c.xfer([s_xnD[i]], s_psD[i], "t", ftr)
                def fev(e, tt=tt, half=half, i=i):
                    for d4 in range(4):
                        dc = half * 4 + d4
                        inst = e.tensor_copy(
                            xnTf[:, tt * 1024 + dc * 128: tt * 1024 + (dc + 1) * 128],
                            psD[i][:, d4 * 128:(d4 + 1) * 128])
                    return inst
                c.xfer([s_psD[i]], s_xnTf[tt], "v", fev,
                       first=(half == 0), last=(half == 1))
            def fcast(e, tt=tt, i=i):
                e.tensor_copy(xhD[:, i * 1024:(i + 1) * 1024],
                              xnTf[:, tt * 1024:(tt + 1) * 1024])
                return e.tensor_tensor(xlD[:, i * 1024:(i + 1) * 1024],
                                       xnTf[:, tt * 1024:(tt + 1) * 1024],
                                       xhD[:, i * 1024:(i + 1) * 1024], ALU.subtract)
            c.xfer([s_xnTf[tt]], s_xhD[i], "v", fcast)
            def fsx(e, tt=tt, i=i):
                for dc in range(NDC):
                    e.dma_start(
                        out=dt["xh_s"][dc * 128:(dc + 1) * 128, tt * 128:(tt + 1) * 128],
                        in_=xhD[:, i * 1024 + dc * 128: i * 1024 + (dc + 1) * 128])
                    inst = e.dma_start(
                        out=dt["xl_s"][dc * 128:(dc + 1) * 128, tt * 128:(tt + 1) * 128],
                        in_=xlD[:, i * 1024 + dc * 128: i * 1024 + (dc + 1) * 128])
                return inst
            c.consume([s_xhD[i]], "g", fsx, inc="d")
        for tt in range(NT):
            i = tt % 2
            def fmm(e, tt=tt, i=i):
                for dc in range(NDC):
                    e.matmul(psR[i][:],
                             xnTf[:, tt * 1024 + dc * 128: tt * 1024 + (dc + 1) * 128],
                             rw_sb[:, dc * 64:(dc + 1) * 64],
                             start=(dc == 0), stop=False)
                return e.matmul(psR[i][:], ones[0:1, 0:128], rb_sb[:],
                                start=False, stop=True)
            c.xfer([s_xnTf[tt], s_rw], s_psR[i], "t", fmm)
            ex = rx[:, i * 64:(i + 1) * 64]
            cl = rcol[:, i * 16:(i + 1) * 16]
            msk = mkD[:, i * 128:i * 128 + 64]
            cwm = mkD[:, i * 128 + 64:i * 128 + 128]
            (mx, ngm, m8, thr, s8, s8r, s8c) = (
                cl[:, 0:1], cl[:, 1:2], cl[:, 2:10],
                cl[:, 10:11], cl[:, 11:12], cl[:, 12:13], cl[:, 13:14])
            t1a = c.xfer([s_psR[i]], s_rcol[i], "v",
                         lambda e, i=i, mx=mx: e.reduce_max(mx, psR[i][:], axis=AX.X))
            t1 = c.produce(s_rcol[i], "g",
                           lambda e, mx=mx, ngm=ngm: e.tensor_scalar_mul(ngm, mx, -1.0),
                           waits=(t1a,), first=False)
            t2 = c.xfer([s_psR[i], s_rcol[i]], s_rx[i], "s",
                        lambda e, i=i, ex=ex, ngm=ngm: e.activation(
                            ex, psR[i][:], AF.Exp, bias=ngm), waits=(t1,))
            t3a = c.produce(s_rcol[i], "v", lambda e, ex=ex, m8=m8: e.max(m8, ex),
                            waits=(t2,), first=False)
            t3 = c.produce(s_rcol[i], "g",
                           lambda e, m8=m8, thr=thr: e.tensor_copy(thr, m8[:, 7:8]),
                           waits=(t3a,), first=False)
            def fr3(e, ex=ex, msk=msk, cwm=cwm, thr=thr):
                ea, tb = broadcast_tensor_aps(ex, thr)
                e.tensor_sub(cwm, ea, tb)
                e.tensor_single_scalar(msk, cwm, 0.0, ALU.is_ge)
                return e.tensor_mul(cwm, ex, msk)
            wmk = s_mk[i].cons + [s_mk[i].prod]
            t5a = c.xfer([s_rx[i], s_rcol[i]], s_rcol[i], "v", fr3,
                         waits=tuple(wmk) + (t3,), first=False)
            t5b = c.produce(s_rcol[i], "v",
                            lambda e, cwm=cwm, s8=s8: e.reduce_sum(s8, cwm, axis=AX.X),
                            waits=(t5a,), first=False)
            t5c = c.produce(s_rcol[i], "g",
                            lambda e, s8=s8, s8c=s8c: e.tensor_copy(s8c, s8),
                            waits=(t5b,), first=False)
            t5 = c.produce(s_rcol[i], "v",
                           lambda e, s8c=s8c, s8r=s8r: e.reciprocal(s8r, s8c),
                           waits=(t5c,), first=False)
            s_mk[i].prod = t5
            s_mk[i].cons = []
            t6 = c.xfer([s_mk[i], s_rcol[i]], s_rx[i], "g",
                        lambda e, ex=ex, cwm=cwm, s8r=s8r: e.tensor_scalar_mul(
                            ex, cwm, s8r), waits=(t5,))
            c.xfer([s_rx[i]], s_psD[i], "t",
                   lambda e, i=i, ex=ex: e.transpose(psD[i][0:64, 0:128], ex, eye_sb[:]))
            c.xfer([s_psD[i]], s_cwst[i], "v",
                   lambda e, i=i: e.tensor_copy(cwst[:, i * 128:(i + 1) * 128],
                                                psD[i][0:64, 0:128]))
            c.consume([s_cwst[i]], "g", lambda e, tt=tt, i=i: e.dma_start(
                out=dt["cw_s"][:, tt * 128:(tt + 1) * 128],
                in_=cwst[:, i * 128:(i + 1) * 128]), inc="d")
        b.barrier()
        esD.close()
        if finish(5):
            return nc, gstack, dt

        esE = ExitStack()
        wg1 = [esE.enter_context(nc.sbuf_tensor(f"wg1_{i}", [128, 4096], F32))
               for i in range(2)]
        wg2 = [esE.enter_context(nc.sbuf_tensor(f"wg2_{i}", [128, 4096], F32))
               for i in range(2)]
        w1h = esE.enter_context(nc.sbuf_tensor("w1h", [128, 8192], BF16))
        w1l = esE.enter_context(nc.sbuf_tensor("w1l", [128, 8192], BF16))
        w2h = esE.enter_context(nc.sbuf_tensor("w2h", [128, 8192], BF16))
        w2l = esE.enter_context(nc.sbuf_tensor("w2l", [128, 8192], BF16))
        xhE = esE.enter_context(nc.sbuf_tensor("xhE", [128, 2 * 4096], BF16))
        xlE = esE.enter_context(nc.sbuf_tensor("xlE", [128, 4096], BF16))
        H_sb = esE.enter_context(nc.sbuf_tensor("H_sb", [128, 2 * 512], F32))
        Hh = esE.enter_context(nc.sbuf_tensor("Hh", [128, 8 * 512], BF16))
        Hl = esE.enter_context(nc.sbuf_tensor("Hl", [128, 8 * 512], BF16))
        accE = esE.enter_context(nc.sbuf_tensor("accE", [128, 4 * 1024], F32))
        cwr = esE.enter_context(nc.sbuf_tensor("cwrE", [1, 4 * 512], F32))
        cwl = esE.enter_context(nc.sbuf_tensor("cwlE", [64, 512], F32))
        b1_sb = esE.enter_context(nc.sbuf_tensor("b1E", [128, 128], F32))
        sb1_sb = esE.enter_context(nc.sbuf_tensor("sb1E", [128, 32], F32))
        b2_sb = esE.enter_context(nc.sbuf_tensor("b2E", [64, 1024], F32))
        sb2_sb = esE.enter_context(nc.sbuf_tensor("sb2E", [1, 1024], F32))
        psH = [esE.enter_context(nc.psum_tensor(f"psH{i}", [128, 512], F32))
               for i in range(2)]
        psCW = [esE.enter_context(nc.psum_tensor(f"psCW{i}", [128, 512], F32))
                for i in range(2)]
        psM = [esE.enter_context(nc.psum_tensor(f"psM{i}", [128, 512], F32))
               for i in range(2)]

        s_wg1 = slots(2)
        s_wg2 = slots(2)
        s_w1b = slots(2)
        s_w2b = slots(2)
        s_xE = slots(2)
        s_xl = Slot()
        s_H = slots(2)
        s_Hb = slots(8)
        s_accT = [[Slot() for _ in range(4)] for _ in range(1)]
        s_accD = slots(NTS)
        s_cwr = slots(4)
        s_cwl = slots(2)
        s_psH = slots(2)
        s_psCW = slots(2)
        s_psM = slots(2)
        s_cst = Slot()
        s_sb2 = Slot()

        def fc(e):
            e.dma_start(out=b1_sb[:], in_=dt["b1T"][:, :])
            e.dma_start(out=sb1_sb[:], in_=dt["sb1T"][:, :])
            return e.dma_start(out=b2_sb[:], in_=dt["b2"][:, :])
        c.produce(s_cst, "y", fc, inc="d")

        NGRP = 36

        def grp_info(g):
            if g < 32:
                return ("r", [2 * g, 2 * g + 1], 4, 0)
            se, half = (g - 32) // 2, (g - 32) % 2
            return ("s", [se], 8, half * 8)

        def cast_w(i):
            def f1(e, i=i):
                e.tensor_copy(w1h[:, i * 4096:(i + 1) * 4096], wg1[i][:])
                return e.tensor_tensor(w1l[:, i * 4096:(i + 1) * 4096], wg1[i][:],
                                       w1h[:, i * 4096:(i + 1) * 4096], ALU.subtract)
            c.xfer([s_wg1[i]], s_w1b[i], "v", f1)
            def f2(e, i=i):
                e.tensor_copy(w2h[:, i * 4096:(i + 1) * 4096], wg2[i][:])
                return e.tensor_tensor(w2l[:, i * 4096:(i + 1) * 4096], wg2[i][:],
                                       w2h[:, i * 4096:(i + 1) * 4096], ALU.subtract)
            c.xfer([s_wg2[i]], s_w2b[i], "v", f2)

        def load_group(g):
            kind, exps, nhc, hoff = grp_info(g)
            i = g % 2
            if kind == "r":
                def f1(e, exps=exps, i=i):
                    for k, ex_ in enumerate(exps):
                        for dc in range(NDC):
                            inst = e.dma_start(
                                out=wg1[i][:, (k * 8 + dc) * 256:(k * 8 + dc + 1) * 256],
                                in_=dt["w1"][ex_, dc * 128:(dc + 1) * 128, :])
                    return inst
                def f2(e, exps=exps, i=i):
                    for k, ex_ in enumerate(exps):
                        for hc in range(2):
                            inst = e.dma_start(
                                out=wg2[i][:, (k * 2 + hc) * 1024:(k * 2 + hc + 1) * 1024],
                                in_=dt["w2"][ex_, hc * 128:(hc + 1) * 128, :])
                    return inst
                c.produce(s_wg1[i], "y", f1, inc="d")
                c.produce(s_wg2[i], "y", f2, inc="d")
                cast_w(i)
            else:
                se = exps[0]
                def f1(e, se=se, hoff=hoff):
                    for dc in range(NDC):
                        idx = dc * 1024
                        inst = e.dma_start(
                            out=wg1[idx // 4096][:, idx % 4096: idx % 4096 + 1024],
                            in_=dt["sw1"][se, dc * 128:(dc + 1) * 128,
                                          hoff * 128: hoff * 128 + 1024])
                    return inst
                def f2(e, se=se, hoff=hoff):
                    for hc in range(8):
                        idx = hc * 1024
                        inst = e.dma_start(
                            out=wg2[idx // 4096][:, idx % 4096: idx % 4096 + 1024],
                            in_=dt["sw2"][se, (hoff + hc) * 128:(hoff + hc + 1) * 128, :])
                    return inst
                tok1 = b.emit("y", f1, waits=s_wg1[0].cons + s_wg1[1].cons +
                              [s_wg1[0].prod, s_wg1[1].prod], inc="d",
                              grp=id(s_wg1[0]))
                s_wg1[0].prod = s_wg1[1].prod = tok1
                s_wg1[0].cons = []
                s_wg1[1].cons = []
                tok2 = b.emit("y", f2, waits=s_wg2[0].cons + s_wg2[1].cons +
                              [s_wg2[0].prod, s_wg2[1].prod], inc="d",
                              grp=id(s_wg2[0]))
                s_wg2[0].prod = s_wg2[1].prod = tok2
                s_wg2[0].cons = []
                s_wg2[1].cons = []
                cast_w(0)
                cast_w(1)
                if hoff == 0:
                    c.produce(s_sb2, "y", lambda e, se=se: e.dma_start(
                        out=sb2_sb[:], in_=dt["sb2r"][0:1, se * 1024:(se + 1) * 1024]),
                        inc="d")

        def w1_tile(wt, g, k, dc, hc):
            if g < 32:
                base = (g % 2) * 4096 + (k * 8 + dc) * 256 + hc * 128
                return wt[:, base:base + 128]
            idx = dc * 1024 + hc * 128
            return wt[:, idx:idx + 128]

        def w2_tile(wt, g, k, hc, dc2):
            if g < 32:
                base = (g % 2) * 4096 + (k * 2 + hc) * 1024 + dc2 * 512
                return wt[:, base:base + 512]
            idx = hc * 1024 + dc2 * 512
            return wt[:, idx:idx + 512]

        load_group(0)
        for g in range(NGRP):
            kind, exps, nhc, hoff = grp_info(g)
            ig = g % 2
            prefetch_now = (g + 1 < NGRP and kind == "r"
                            and grp_info(g + 1)[0] == "r")
            if prefetch_now:
                load_group(g + 1)
            for ts in range(NTS):
                it = ts % 2
                def fx(e, ts=ts, it=it):
                    for dc in range(NDC):
                        inst = e.dma_start(
                            out=xhE[:, it * 4096 + dc * 512: it * 4096 + (dc + 1) * 512],
                            in_=dt["xh_s"][dc * 128:(dc + 1) * 128,
                                           ts * 512:(ts + 1) * 512])
                    return inst
                c.produce(s_xE[it], "y", fx, inc="d")
                def fxl(e, ts=ts):
                    for dc in range(NDC):
                        inst = e.dma_start(
                            out=xlE[:, dc * 512:(dc + 1) * 512],
                            in_=dt["xl_s"][dc * 128:(dc + 1) * 128,
                                           ts * 512:(ts + 1) * 512])
                    return inst
                c.produce(s_xl, "y", fxl, inc="d")
                if g > 0:
                    def fa(e, ts=ts, it=it):
                        for tt in range(4):
                            inst = e.dma_start(
                                out=accE[:, tt * 1024:(tt + 1) * 1024],
                                in_=dt["acc_s"][ts * 512 + tt * 128:
                                                ts * 512 + (tt + 1) * 128, :])
                        return inst
                    wl = [s_accD[ts].prod]
                    for tt2 in range(4):
                        wl += s_accT[0][tt2].cons + [s_accT[0][tt2].prod]
                    tok = b.emit("y", fa, waits=wl, inc="d",
                                 grp=id(s_accT[0][0]))
                    for tt in range(4):
                        s_accT[0][tt].prod = tok
                        s_accT[0][tt].cons = []
                    s_accD[ts].cons.append(tok)
                def emit_mm1(hcol, g, k, hc, bias_ap, cwslot, cwbank):
                    ih = hcol % 2
                    def fm1(e, g=g, k=k, hc=hc, it=it, ih=ih):
                        n = 0
                        for dc in range(NDC):
                            for ph in range(3):
                                wt = (w1h, w1l, w1h)[ph]
                                xo = it * 4096 if ph < 2 else 0
                                xt = (xhE, xhE, xlE)[ph]
                                inst = e.matmul(
                                    psH[ih][:], w1_tile(wt, g, k, dc, hc),
                                    xt[:, xo + dc * 512: xo + (dc + 1) * 512],
                                    start=(n == 0), stop=(n == 23))
                                n += 1
                        return inst
                    srcs = [s_xE[it], s_xl, s_w1b[ig]] if g < 32 else \
                           [s_xE[it], s_xl, s_w1b[0], s_w1b[1]]
                    c.xfer(srcs, s_psH[ih], "t", fm1)
                    hs = hcol % 2
                    c.xfer([s_psH[ih], s_cst], s_H[hs], "s",
                           lambda e, ih=ih, hs=hs, bias_ap=bias_ap:
                           e.activation(H_sb[:, hs * 512:(hs + 1) * 512],
                                        psH[ih][:], AF.Silu, bias=bias_ap))
                    if cwslot is not None:
                        c.xfer([cwslot, s_H[hs]], s_H[hs], "v",
                               lambda e, hs=hs, cwbank=cwbank: e.tensor_mul(
                                   H_sb[:, hs * 512:(hs + 1) * 512],
                                   H_sb[:, hs * 512:(hs + 1) * 512],
                                   cwbank[:]),
                               first=False, last=True)
                    def fhc(e, hcol=hcol, hs=hs):
                        e.tensor_copy(Hh[:, hcol * 512:(hcol + 1) * 512],
                                      H_sb[:, hs * 512:(hs + 1) * 512])
                        return e.tensor_tensor(
                            Hl[:, hcol * 512:(hcol + 1) * 512],
                            H_sb[:, hs * 512:(hs + 1) * 512],
                            Hh[:, hcol * 512:(hcol + 1) * 512], ALU.subtract)
                    c.xfer([s_H[hs], s_Hb[hcol]], s_Hb[hcol], "v", fhc)

                if kind == "r":
                    def load_cw(tsx):
                        for k2, ex2 in enumerate(exps):
                            ic2 = (tsx * 2 + k2) % 4
                            c.produce(s_cwr[ic2], "s",
                                      lambda e, ex2=ex2, tsx=tsx, ic2=ic2:
                                      e.dma_start(
                                          out=cwr[:, ic2 * 512:(ic2 + 1) * 512],
                                          in_=dt["cw_s"][ex2:ex2 + 1,
                                                         tsx * 512:(tsx + 1) * 512]),
                                      inc="d")
                    if ts == 0:
                        load_cw(0)
                        load_cw(1)
                    elif ts + 1 < NTS:
                        load_cw(ts + 1)
                    for k, ex_ in enumerate(exps):
                        ic = (ts * 2 + k) % 4
                        c.xfer([s_cwr[ic]], s_psCW[ic % 2], "t",
                               lambda e, ic=ic: e.matmul(
                                   psCW[ic % 2][:], ones[0:1, 0:128],
                                   cwr[:, ic * 512:(ic + 1) * 512],
                                   start=True, stop=True))
                        for hc in range(2):
                            emit_mm1(k * 2 + hc, g, k, hc,
                                     b1_sb[:, hc * 64 + ex_:hc * 64 + ex_ + 1],
                                     s_psCW[ic % 2], psCW[ic % 2])
                else:
                    se = exps[0]
                    for hc in range(8):
                        emit_mm1(hc, g, 0, hc,
                                 sb1_sb[:, se * 16 + hoff + hc:se * 16 + hoff + hc + 1],
                                 None, None)
                if g == NGRP - 1:
                    c.produce(s_cwl[0], "y", lambda e, ts=ts: e.dma_start(
                        out=cwl[:, 0:512],
                        in_=dt["cw_s"][:, ts * 512:(ts + 1) * 512]), inc="d")
                sb2_here = (kind == "s" and hoff == 0)
                b2_here = (g == NGRP - 1)
                for tt in range(4):
                    for dc2 in range(2):
                        im = (tt * 2 + dc2) % 2
                        tot = nhc * 3 + (1 if sb2_here else 0) + (1 if b2_here else 0)
                        def fm2(e, g=g, tt=tt, dc2=dc2, it=it, im=im, nhc=nhc,
                                kind=kind, exps=exps, tot=tot,
                                sb2_here=sb2_here, b2_here=b2_here):
                            n = 0
                            for hcol in range(nhc):
                                k, hc = ((hcol // 2, hcol % 2) if kind == "r"
                                         else (0, hcol))
                                for ph in range(3):
                                    ht = (Hh, Hl, Hh)[ph]
                                    wt = (w2h, w2h, w2l)[ph]
                                    inst = e.matmul(
                                        psM[im][:],
                                        ht[:, hcol * 512 + tt * 128:
                                           hcol * 512 + (tt + 1) * 128],
                                        w2_tile(wt, g, k, hc, dc2),
                                        start=(n == 0), stop=(n == tot - 1))
                                    n += 1
                            if sb2_here:
                                inst = e.matmul(
                                    psM[im][:], ones[0:1, 0:128],
                                    sb2_sb[0:1, dc2 * 512:(dc2 + 1) * 512],
                                    start=False, stop=(n == tot - 1))
                                n += 1
                            if b2_here:
                                inst = e.matmul(
                                    psM[im][:],
                                    cwl[:, tt * 128:(tt + 1) * 128],
                                    b2_sb[:, dc2 * 512:(dc2 + 1) * 512],
                                    start=False, stop=True)
                            return inst
                        srcs = list(s_Hb[:nhc])
                        if kind == "r":
                            srcs.append(s_w2b[ig])
                        else:
                            srcs += [s_w2b[0], s_w2b[1]]
                        if sb2_here:
                            srcs.append(s_sb2)
                        if b2_here:
                            srcs.append(s_cwl[0])
                        c.xfer(srcs, s_psM[im], "t", fm2)
                        aap = accE[:, tt * 1024 + dc2 * 512:
                                   tt * 1024 + (dc2 + 1) * 512]
                        if g == 0:
                            c.xfer([s_psM[im]], s_accT[0][tt], "v",
                                   lambda e, aap=aap, im=im: e.tensor_copy(
                                       aap, psM[im][:]),
                                   first=(dc2 == 0), last=(dc2 == 1))
                        else:
                            c.xfer([s_psM[im]], s_accT[0][tt], "v",
                                   lambda e, aap=aap, im=im: e.tensor_add(
                                       aap, aap, psM[im][:]),
                                   first=(dc2 == 0), last=(dc2 == 1))
                def fst(e, ts=ts, it=it):
                    for tt in range(4):
                        inst = e.dma_start(
                            out=dt["acc_s"][ts * 512 + tt * 128:ts * 512 + (tt + 1) * 128, :],
                            in_=accE[:, tt * 1024:(tt + 1) * 1024])
                    return inst
                tok = b.emit("s", fst, waits=[s_accT[0][tt].prod for tt in range(4)] +
                             s_accD[ts].cons + [s_accD[ts].prod], inc="d",
                             grp=id(s_accD[ts]))
                for tt in range(4):
                    s_accT[0][tt].cons.append(tok)
                s_accD[ts].prod = tok
                s_accD[ts].cons = []
            if g + 1 < NGRP and not prefetch_now:
                load_group(g + 1)
        b.barrier()
        esE.close()
        if finish(6):
            return nc, gstack, dt

        esF = ExitStack()
        aF = esF.enter_context(nc.sbuf_tensor("aF", [128, 2 * 1024], F32))
        xF = esF.enter_context(nc.sbuf_tensor("xF", [128, 2 * 1024], F32))
        lnF_o = esF.enter_context(nc.sbuf_tensor("lnFo", [128, 1024], F32))
        x2F = esF.enter_context(nc.sbuf_tensor("x2F", [128, 2 * 1024], F32))
        lnF = LNKit(nc, esF, "F")
        s_aF = [_attach(Slot(), aF[:, i * 1024:(i + 1) * 1024]) for i in range(2)]
        s_xF = [_attach(Slot(), xF[:, i * 1024:(i + 1) * 1024]) for i in range(2)]
        s_lnF = _attach(Slot(), lnF_o[:])
        s_x2F = [_attach(Slot(), x2F[:, i * 1024:(i + 1) * 1024]) for i in range(2)]

        def loadF(tt):
            i = tt % 2
            c.produce(s_aF[i], "y", lambda e, tt=tt, i=i: e.dma_start(
                out=s_aF[i].ap, in_=dt["acc_s"][tt * 128:(tt + 1) * 128, :]), inc="d")
            c.produce(s_xF[i], "g", lambda e, tt=tt, i=i: e.dma_start(
                out=s_xF[i].ap, in_=dt["x_s"][tt * 128:(tt + 1) * 128, :]), inc="d")

        loadF(0)
        for tt in range(NT):
            i = tt % 2
            if tt + 1 < NT:
                loadF(tt + 1)
            lnF.run(c, s_aF[i], s_lnF.ap, s_lnF)
            c.xfer([s_lnF, s_xF[i]], s_x2F[i], "v",
                   lambda e, i=i: e.tensor_add(s_x2F[i].ap, s_xF[i].ap, lnF_o[:]))
            c.consume([s_x2F[i]], "s", lambda e, tt=tt, i=i: e.dma_start(
                out=dt["x_s"][tt * 128:(tt + 1) * 128, :], in_=s_x2F[i].ap), inc="d")
        b.barrier()
        esF.close()
        if finish(7):
            return nc, gstack, dt

        esG = ExitStack()
        w3 = esG.enter_context(nc.sbuf_tensor("w3G", [128, 3 * 8 * 1024], F32))
        xG = esG.enter_context(nc.sbuf_tensor("xG", [128, 4 * 1024], F32))
        xnG = esG.enter_context(nc.sbuf_tensor("xnG", [128, 4 * 1024], F32))
        xnTG = esG.enter_context(nc.sbuf_tensor("xnTG", [128, 2 * 4096], F32))
        pTg = esG.enter_context(nc.sbuf_tensor("pTg", [128, 2 * 512], F32))
        ptok = esG.enter_context(nc.sbuf_tensor("ptokG", [128, 4 * 1024], F32))
        sqG = esG.enter_context(nc.sbuf_tensor("sqG", [128, 1024], F32))
        colG = esG.enter_context(nc.sbuf_tensor("colG", [128, 64], F32))
        qnT = esG.enter_context(nc.sbuf_tensor("qnTG", [128, 8 * 512], F32))
        lnG = LNKit(nc, esG, "G")
        psG = [esG.enter_context(nc.psum_tensor(f"psG{i}", [128, 512], F32))
               for i in range(2)]
        psG2 = [esG.enter_context(nc.psum_tensor(f"psG2{i}", [128, 512], F32))
                for i in range(2)]

        s_w3 = Slot()
        s_xG = [_attach(Slot(), xG[:, tt * 1024:(tt + 1) * 1024])
                for tt in range(4)]
        s_xnG = [_attach(Slot(), xnG[:, tt * 1024:(tt + 1) * 1024]) for tt in range(4)]
        s_xnTG = [[Slot() for _ in range(NDC)] for _ in range(2)]
        s_psG = slots(2)
        s_psG2 = slots(2)
        s_pT = slots(2)
        s_ptok = [_attach(Slot(), ptok[:, tt * 1024:(tt + 1) * 1024]) for tt in range(4)]
        s_sq = Slot()
        s_col = Slot()
        s_qnT = slots(8)

        def fw3(e):
            for w in range(3):
                nm = ("wq1T", "wk1T", "wv1T")[w]
                for dc in range(NDC):
                    inst = e.dma_start(
                        out=w3[:, (w * 8 + dc) * 1024:(w * 8 + dc + 1) * 1024],
                        in_=dt[nm][dc * 128:(dc + 1) * 128, :])
            return inst
        c.produce(s_w3, "y", fw3, inc="d")

        def loadG(ts):
            for tt in range(4):
                c.produce(s_xG[tt], "y", lambda e, ts=ts, tt=tt: e.dma_start(
                    out=s_xG[tt].ap,
                    in_=dt["x_s"][ts * 512 + tt * 128:ts * 512 + (tt + 1) * 128, :]),
                    inc="d")

        for ts in range(DEV.get("G_ts", NTS)):
            i = ts % 2
            loadG(ts)
            for tt in range(4):
                lnG.run(c, s_xG[tt], s_xnG[tt].ap, s_xnG[tt])
            for dc in range(NDC):
                def ftr(e, dc=dc):
                    for tt in range(4):
                        inst = e.transpose(
                            psG[dc % 2][:, tt * 128:(tt + 1) * 128],
                            xnG[:, tt * 1024 + dc * 128: tt * 1024 + (dc + 1) * 128],
                            eye_sb[:])
                    return inst
                c.xfer(s_xnG, s_psG[dc % 2], "t", ftr)
                c.xfer([s_psG[dc % 2]], s_xnTG[i][dc], "v",
                       lambda e, dc=dc, i=i: e.tensor_copy(
                           xnTG[:, i * 4096 + dc * 512: i * 4096 + (dc + 1) * 512],
                           psG[dc % 2][:]))
            for w in range(DEV.get("G_w", 3)):
                def emit_tr(ec):
                    pe = ec % 2
                    for tt in range(4):
                        c.xfer([s_pT[pe]], s_psG[tt % 2], "t",
                               lambda e, tt=tt, pe=pe: e.transpose(
                                   psG[tt % 2][:, (tt // 2) * 128:(tt // 2) * 128 + 128],
                                   pTg[:, pe * 512 + tt * 128: pe * 512 + (tt + 1) * 128],
                                   eye_sb[:]))
                        c.xfer([s_psG[tt % 2]], s_ptok[tt], "v",
                               lambda e, tt=tt, ec=ec: e.tensor_copy(
                                   ptok[:, tt * 1024 + ec * 128: tt * 1024 + (ec + 1) * 128],
                                   psG[tt % 2][:, (tt // 2) * 128:(tt // 2) * 128 + 128]),
                               first=(ec == 0), last=(ec == 7))
                for ec in range(8):
                    def fmm(e, w=w, ec=ec, i=i):
                        for dc in range(NDC):
                            inst = e.matmul(
                                psG2[ec % 2][:],
                                w3[:, (w * 8 + dc) * 1024 + ec * 128:
                                   (w * 8 + dc) * 1024 + (ec + 1) * 128],
                                xnTG[:, i * 4096 + dc * 512: i * 4096 + (dc + 1) * 512],
                                start=(dc == 0), stop=(dc == NDC - 1))
                        return inst
                    c.xfer(s_xnTG[i] + [s_w3], s_psG2[ec % 2], "t", fmm)
                    c.xfer([s_psG2[ec % 2]], s_pT[ec % 2], "v",
                           lambda e, ec=ec: e.tensor_copy(
                               pTg[:, (ec % 2) * 512:(ec % 2 + 1) * 512],
                               psG2[ec % 2][:]))
                    if ec >= 1:
                        emit_tr(ec - 1)
                emit_tr(7)
                for tt in range(4):
                    pa = s_ptok[tt].ap
                    if w < 2:
                        t1 = c.xfer([s_ptok[tt]], s_sq, "v",
                                    lambda e, pa=pa: e.tensor_mul(sqG[:], pa, pa))
                        def fss(e):
                            for h in range(16):
                                inst = e.reduce_sum(colG[:, h:h + 1],
                                                    sqG[:, h * 64:(h + 1) * 64],
                                                    axis=AX.X)
                            return inst
                        t2 = c.xfer([s_sq], s_col, "v", fss, waits=(t1,))
                        t3 = c.produce(s_col, "g",
                                       lambda e: e.tensor_scalar_max(
                                           colG[:, 16:32], colG[:, 0:16], 1e-24),
                                       waits=(t2,), first=False)
                        t4a = c.produce(s_col, "v",
                                        lambda e: e.reciprocal(colG[:, 32:48],
                                                               colG[:, 16:32]),
                                        waits=(t3,), first=False)
                        t4 = c.produce(s_col, "s",
                                       lambda e: e.activation(colG[:, 48:64],
                                                              colG[:, 32:48], AF.Sqrt),
                                       waits=(t4a,), first=False)
                        def fap(e, pa=pa):
                            for h in range(16):
                                inst = e.tensor_scalar_mul(
                                    pa[:, h * 64:(h + 1) * 64],
                                    pa[:, h * 64:(h + 1) * 64],
                                    colG[:, 48 + h:49 + h])
                            return inst
                        c.xfer([s_col, s_ptok[tt]], s_ptok[tt], "v", fap)
                    if w == 0:
                        for ec in range(8):
                            c.xfer([s_ptok[tt]], s_psG[ec % 2], "t",
                                   lambda e, ec=ec, pa=pa: e.transpose(
                                       psG[ec % 2][:, 0:128],
                                       pa[:, ec * 128:(ec + 1) * 128], eye_sb[:]))
                            c.xfer([s_psG[ec % 2]], s_qnT[ec], "v",
                                   lambda e, ec=ec, tt=tt: e.tensor_copy(
                                       qnT[:, ec * 512 + tt * 128: ec * 512 + (tt + 1) * 128],
                                       psG[ec % 2][:, 0:128]),
                                   first=(tt == 0), last=(tt == 3))
                    else:
                        dst = dt["k2_s"] if (w == 1 or DEV.get("G_vk")) else dt["v2_s"]
                        c.consume([s_ptok[tt]], "g",
                                  lambda e, dst=dst, ts=ts, tt=tt, pa=pa: e.dma_start(
                                      out=dst[ts * 512 + tt * 128:ts * 512 + (tt + 1) * 128, :],
                                      in_=pa), inc="d")
                if w == 0:
                    def fq(e, ts=ts):
                        for ec in range(8):
                            inst = e.dma_start(
                                out=dt["q2T_s"][ec * 128:(ec + 1) * 128,
                                                ts * 512:(ts + 1) * 512],
                                in_=qnT[:, ec * 512:(ec + 1) * 512])
                        return inst
                    c.consume(s_qnT, "g", fq, inc="d")
        b.barrier()
        esG.close()
        if finish(8):
            return nc, gstack, dt

        esH = ExitStack()
        fac_r = esH.enter_context(nc.sbuf_tensor("facH", [1, 64], F32))
        fac64 = esH.enter_context(nc.sbuf_tensor("fac64H", [64, 16], F32))
        kvH = esH.enter_context(nc.sbuf_tensor("kvH", [128, 8 * 64], F32))
        ktH = esH.enter_context(nc.sbuf_tensor("ktH", [128, 2 * 1024], F32))
        vtH = esH.enter_context(nc.sbuf_tensor("vtH", [128, 2 * 1024], F32))
        qnH = esH.enter_context(nc.sbuf_tensor("qnH", [128, 2 * 4096], F32))
        atH = esH.enter_context(nc.sbuf_tensor("atH", [128, 4096], F32))
        woH = esH.enter_context(nc.sbuf_tensor("woH", [128, 8 * 1024], F32))
        aTH = esH.enter_context(nc.sbuf_tensor("aTH", [128, 4096], F32))
        atokH = esH.enter_context(nc.sbuf_tensor("atokH", [128, 2 * 1024], F32))
        xH = esH.enter_context(nc.sbuf_tensor("xH", [128, 2 * 4 * 1024], F32))
        lnoH = esH.enter_context(nc.sbuf_tensor("lnoH", [128, 1024], F32))
        x3H = esH.enter_context(nc.sbuf_tensor("x3H", [128, 2 * 1024], F32))
        lnH = LNKit(nc, esH, "H")
        psKV = esH.enter_context(nc.psum_tensor("psKV", [64, 1024], F32))
        psAT = [esH.enter_context(nc.psum_tensor(f"psAT{i}", [64, 512], F32))
                for i in range(2)]
        psP = [esH.enter_context(nc.psum_tensor(f"psHP{i}", [128, 512], F32))
               for i in range(2)]
        psF = esH.enter_context(nc.psum_tensor("psHF", [64, 16], F32))

        s_facr = Slot()
        s_fac = Slot()
        s_kv = Slot()
        s_kvsb = Slot()
        s_kt = slots(2)
        s_vt = slots(2)
        s_qn = slots(2)
        s_at = slots(8)
        s_wo = Slot()
        s_psAT = slots(2)
        s_psP = slots(2)
        s_psF = Slot()
        s_aT = slots(8)
        s_atok = [_attach(Slot(), atokH[:, i * 1024:(i + 1) * 1024]) for i in range(2)]
        s_xH = [[_attach(Slot(), xH[:, (i * 4 + tt) * 1024:(i * 4 + tt + 1) * 1024])
                 for tt in range(4)] for i in range(2)]
        s_lnH = _attach(Slot(), lnoH[:])
        s_x3 = [_attach(Slot(), x3H[:, i * 1024:(i + 1) * 1024]) for i in range(2)]

        t_dm = b.emit("y", lambda e: e.dma_start(out=fac_r[0:1, 0:16],
                                                 in_=dt["ncst"][:, :]), inc="d")
        t_ms = b.emit("v", lambda e: e.memset(fac_r[0:1, 32:48], 4096.0), inc="c")
        t_sg = b.emit("s", lambda e: e.activation(fac_r[0:1, 16:32], fac_r[0:1, 0:16],
                                                  AF.Sigmoid), waits=(t_dm,), inc="c")
        t_ln = b.emit("s", lambda e: e.activation(fac_r[0:1, 32:48], fac_r[0:1, 32:48],
                                                  AF.Ln), waits=(t_ms,), inc="c")
        def ffin(e):
            e.tensor_mul(fac_r[0:1, 48:64], fac_r[0:1, 16:32], fac_r[0:1, 32:48])
            return e.tensor_scalar_mul(fac_r[0:1, 0:16], fac_r[0:1, 48:64], -1.0)
        t_mu = b.emit("g", ffin, waits=(t_sg, t_ln), inc="c")
        t_ex = b.emit("s", lambda e: e.activation(fac_r[0:1, 16:32], fac_r[0:1, 0:16],
                                                  AF.Exp), waits=(t_mu,), inc="c")
        t_bc = b.emit("t", lambda e: e.matmul(psF[:, :], ones[0:1, 0:64],
                                              fac_r[0:1, 16:32], start=True, stop=True),
                      waits=(t_ex,), inc="c")
        t_f6 = b.emit("v", lambda e: e.tensor_copy(fac64[:], psF[:, :]),
                      waits=(t_bc,), inc="c")
        s_fac.prod = t_f6

        def fwo(e):
            for ec in range(NDC):
                inst = e.dma_start(out=woH[:, ec * 1024:(ec + 1) * 1024],
                                   in_=dt["wo1T"][ec * 128:(ec + 1) * 128, :])
            return inst
        c.produce(s_wo, "y", fwo, inc="d")

        def loadkv(tt):
            i = tt % 2
            c.produce(s_kt[i], "y", lambda e, tt=tt, i=i: e.dma_start(
                out=ktH[:, i * 1024:(i + 1) * 1024],
                in_=dt["k2_s"][tt * 128:(tt + 1) * 128, :]), inc="d")
            c.produce(s_vt[i], "y", lambda e, tt=tt, i=i: e.dma_start(
                out=vtH[:, i * 1024:(i + 1) * 1024],
                in_=dt["v2_s"][tt * 128:(tt + 1) * 128, :]), inc="d")

        loadkv(0)
        for tt in range(NT):
            i = tt % 2
            if tt + 1 < NT:
                loadkv(tt + 1)
            def fkv(e, tt=tt, i=i):
                for h in range(16):
                    inst = e.matmul(
                        psKV[:, h * 64:(h + 1) * 64],
                        ktH[:, i * 1024 + h * 64: i * 1024 + (h + 1) * 64],
                        vtH[:, i * 1024 + h * 64: i * 1024 + (h + 1) * 64],
                        start=(tt == 0 and h % 8 == 0), stop=(tt == NT - 1))
                return inst
            c.xfer([s_kt[i], s_vt[i]], s_kv, "t", fkv,
                   first=(tt == 0), last=(tt == NT - 1))
        def fkve(e):
            for h in range(16):
                inst = e.activation(
                    kvH[(h % 2) * 64:(h % 2) * 64 + 64,
                        (h // 2) * 64:(h // 2 + 1) * 64],
                    psKV[:, h * 64:(h + 1) * 64], AF.Copy,
                    scale=fac64[0:64, h:h + 1])
            return inst
        c.xfer([s_kv, s_fac], s_kvsb, "s", fkve)

        def loadqn(ts):
            i = ts % 2
            def fq(e, ts=ts, i=i):
                for ec in range(8):
                    inst = e.dma_start(
                        out=qnH[:, i * 4096 + ec * 512: i * 4096 + (ec + 1) * 512],
                        in_=dt["q2T_s"][ec * 128:(ec + 1) * 128, ts * 512:(ts + 1) * 512])
                return inst
            c.produce(s_qn[i], "y", fq, inc="d")
            for tt in range(4):
                c.produce(s_xH[i][tt], "y", lambda e, ts=ts, tt=tt, i=i: e.dma_start(
                    out=s_xH[i][tt].ap,
                    in_=dt["x_s"][ts * 512 + tt * 128:ts * 512 + (tt + 1) * 128, :]),
                    inc="d")

        loadqn(0)
        for ts in range(NTS):
            i = ts % 2
            if ts + 1 < NTS:
                loadqn(ts + 1)
            for h2 in range(8):
                def fat(e, h2=h2, i=i):
                    e.matmul(psAT[0][:, :],
                             kvH[0:64, h2 * 64:(h2 + 1) * 64],
                             qnH[0:64, i * 4096 + h2 * 512: i * 4096 + (h2 + 1) * 512],
                             start=True, stop=True, tile_position=(0, 0))
                    return e.matmul(psAT[1][:, :],
                                    kvH[64:128, h2 * 64:(h2 + 1) * 64],
                                    qnH[64:128, i * 4096 + h2 * 512:
                                        i * 4096 + (h2 + 1) * 512],
                                    start=True, stop=True, tile_position=(64, 0))
                w = [s_kvsb.prod, s_qn[i].prod] + s_psAT[0].cons + s_psAT[1].cons + \
                    [s_psAT[0].prod, s_psAT[1].prod]
                tk = b.emit("t", fat, waits=w, inc="c")
                s_kvsb.cons.append(tk); s_qn[i].cons.append(tk)
                s_psAT[0].prod = tk; s_psAT[0].cons = []
                s_psAT[1].prod = tk; s_psAT[1].cons = []
                def fev(e, h2=h2):
                    e.tensor_copy(atH[0:64, h2 * 512:(h2 + 1) * 512], psAT[0][:, :])
                    return e.tensor_copy(atH[64:128, h2 * 512:(h2 + 1) * 512],
                                         psAT[1][:, :])
                c.xfer([s_psAT[0], s_psAT[1]], s_at[h2], "v", fev)
            if DEV.get("H_dbg"):
                def fdbg(e, ts=ts):
                    for ec in range(8):
                        inst = e.dma_start(
                            out=dt["kT_s"][ec * 128:(ec + 1) * 128,
                                           ts * 512:(ts + 1) * 512],
                            in_=atH[:, ec * 512:(ec + 1) * 512])
                    return inst
                c.consume(list(s_at), "g", fdbg, inc="d")
            for oc in range(8):
                def fo(e, oc=oc):
                    for ec in range(8):
                        inst = e.matmul(
                            psP[oc % 2][:],
                            woH[:, ec * 1024 + oc * 128: ec * 1024 + (oc + 1) * 128],
                            atH[:, ec * 512:(ec + 1) * 512],
                            start=(ec == 0), stop=(ec == 7))
                    return inst
                c.xfer(list(s_at) + [s_wo], s_psP[oc % 2], "t", fo)
                c.xfer([s_psP[oc % 2]], s_aT[oc], "v",
                       lambda e, oc=oc: e.tensor_copy(aTH[:, oc * 512:(oc + 1) * 512],
                                                      psP[oc % 2][:]))
            for tt in range(4):
                it = tt % 2
                for half in range(2):
                    def ftr(e, tt=tt, it=it, half=half):
                        for o4 in range(4):
                            oc = half * 4 + o4
                            inst = e.transpose(
                                psP[it][:, o4 * 128:(o4 + 1) * 128],
                                aTH[:, oc * 512 + tt * 128: oc * 512 + (tt + 1) * 128],
                                eye_sb[:])
                        return inst
                    c.xfer(list(s_aT), s_psP[it], "t", ftr)
                    c.xfer([s_psP[it]], s_atok[it], "v",
                           lambda e, it=it, half=half: e.tensor_copy(
                               atokH[:, it * 1024 + half * 512:
                                     it * 1024 + (half + 1) * 512],
                               psP[it][:]),
                           first=(half == 0), last=(half == 1))
                lnH.run(c, s_atok[it], s_lnH.ap, s_lnH)
                c.xfer([s_lnH, s_xH[i][tt]], s_x3[it], "v",
                       lambda e, it=it, i=i, tt=tt: e.tensor_add(
                           s_x3[it].ap, s_xH[i][tt].ap, lnoH[:]))
                c.consume([s_x3[it]], "s", lambda e, ts=ts, tt=tt, it=it: e.dma_start(
                    out=dt["x_s"][ts * 512 + tt * 128:ts * 512 + (tt + 1) * 128, :],
                    in_=s_x3[it].ap), inc="d")
        b.barrier()
        esH.close()
        if finish(9):
            return nc, gstack, dt

        esI = ExitStack()
        xI = esI.enter_context(nc.sbuf_tensor("xI", [128, 2 * 4 * 1024], F32))
        xnI = esI.enter_context(nc.sbuf_tensor("xnI", [128, 4 * 1024], F32))
        xnTI = esI.enter_context(nc.sbuf_tensor("xnTI", [128, 4096], F32))
        w1I = esI.enter_context(nc.sbuf_tensor("w1I", [128, 2 * 1024], F32))
        w2I = esI.enter_context(nc.sbuf_tensor("w2I", [128, 2 * 512], F32))
        hI = esI.enter_context(nc.sbuf_tensor("hI", [128, 32 * 512], F32))
        b2I = esI.enter_context(nc.sbuf_tensor("b2I", [1, 1024], F32))
        b1I = esI.enter_context(nc.sbuf_tensor("b1I", [128, 32], F32))
        oI = esI.enter_context(nc.sbuf_tensor("oI", [128, 4 * 1024], F32))
        lnoI = esI.enter_context(nc.sbuf_tensor("lnoI", [128, 1024], F32))
        yI = esI.enter_context(nc.sbuf_tensor("yI", [128, 2 * 1024], F32))
        lnI = LNKit(nc, esI, "I")
        psI = [esI.enter_context(nc.psum_tensor(f"psI{i}", [128, 512], F32))
               for i in range(2)]
        psJ = [esI.enter_context(nc.psum_tensor(f"psJ{i}", [128, 512], F32))
               for i in range(4)]

        s_xI = [[_attach(Slot(), xI[:, (i * 4 + tt) * 1024:(i * 4 + tt + 1) * 1024])
                 for tt in range(4)] for i in range(2)]
        s_xnI = [_attach(Slot(), xnI[:, tt * 1024:(tt + 1) * 1024]) for tt in range(4)]
        s_xnTI = [Slot() for _ in range(NDC)]
        s_w1 = slots(2)
        s_w2 = slots(2)
        s_h = slots(32)
        s_psI = slots(2)
        s_psJ = slots(4)
        s_o4 = [_attach(Slot(), oI[:, tt_ * 1024:(tt_ + 1) * 1024]) for tt_ in range(4)]
        s_lnI = _attach(Slot(), lnoI[:])
        s_yI = [_attach(Slot(), yI[:, i * 1024:(i + 1) * 1024]) for i in range(2)]
        s_bI = Slot()

        def fb(e):
            e.dma_start(out=b1I[:], in_=dt["b1fT"][:, :])
            return e.dma_start(out=b2I[:], in_=dt["b2f"][:, :])
        c.produce(s_bI, "y", fb, inc="d")

        def loadI(ts):
            i = ts % 2
            for tt in range(4):
                c.produce(s_xI[i][tt], "y", lambda e, ts=ts, tt=tt, i=i: e.dma_start(
                    out=s_xI[i][tt].ap,
                    in_=dt["x_s"][ts * 512 + tt * 128:ts * 512 + (tt + 1) * 128, :]),
                    inc="d")

        loadI(0)
        for ts in range(NTS):
            i = ts % 2
            if ts + 1 < NTS:
                loadI(ts + 1)
            for tt in range(4):
                lnI.run(c, s_xI[i][tt], s_xnI[tt].ap, s_xnI[tt])
            for dc in range(NDC):
                def ftr(e, dc=dc):
                    for tt in range(4):
                        inst = e.transpose(
                            psI[dc % 2][:, tt * 128:(tt + 1) * 128],
                            xnI[:, tt * 1024 + dc * 128: tt * 1024 + (dc + 1) * 128],
                            eye_sb[:])
                    return inst
                c.xfer(s_xnI, s_psI[dc % 2], "t", ftr)
                c.xfer([s_psI[dc % 2]], s_xnTI[dc], "v",
                       lambda e, dc=dc: e.tensor_copy(
                           xnTI[:, dc * 512:(dc + 1) * 512], psI[dc % 2][:]))
            for ic in range(32):
                iw = ic % 2
                def fw(e, ic=ic, iw=iw):
                    for dc in range(NDC):
                        inst = e.dma_start(
                            out=w1I[:, iw * 1024 + dc * 128: iw * 1024 + (dc + 1) * 128],
                            in_=dt["w1f"][dc * 128:(dc + 1) * 128, ic * 128:(ic + 1) * 128])
                    return inst
                c.produce(s_w1[iw], "y", fw, inc="d")
                def fm(e, ic=ic, iw=iw):
                    for dc in range(NDC):
                        inst = e.matmul(
                            psI[iw][:],
                            w1I[:, iw * 1024 + dc * 128: iw * 1024 + (dc + 1) * 128],
                            xnTI[:, dc * 512:(dc + 1) * 512],
                            start=(dc == 0), stop=(dc == NDC - 1))
                    return inst
                c.xfer(s_xnTI + [s_w1[iw]], s_psI[iw], "t", fm)
                c.xfer([s_psI[iw], s_bI], s_h[ic], "v",
                       lambda e, ic=ic, iw=iw: e.tensor_scalar(
                           hI[:, ic * 512:(ic + 1) * 512], psI[iw][:],
                           b1I[:, ic:ic + 1], 0.0, ALU.add, ALU.max))
            for dc2 in range(2):
                for ic in range(32):
                    iw2 = ic % 2
                    c.produce(s_w2[iw2], "g", lambda e, ic=ic, dc2=dc2, iw2=iw2:
                              e.dma_start(out=w2I[:, iw2 * 512:(iw2 + 1) * 512],
                                          in_=dt["w2f"][ic * 128:(ic + 1) * 128,
                                                        dc2 * 512:(dc2 + 1) * 512]),
                              inc="d")
                    def fm2(e, ic=ic, iw2=iw2):
                        for tt in range(4):
                            inst = e.matmul(
                                psJ[tt][:],
                                hI[:, ic * 512 + tt * 128: ic * 512 + (tt + 1) * 128],
                                w2I[:, iw2 * 512:(iw2 + 1) * 512],
                                start=(ic == 0), stop=False)
                        return inst
                    wl = [s_h[ic].prod, s_w2[iw2].prod]
                    if ic == 0:
                        for tt in range(4):
                            wl += s_psJ[tt].cons + [s_psJ[tt].prod]
                    tk = b.emit("t", fm2, waits=wl, inc="c")
                    s_h[ic].cons.append(tk)
                    s_w2[iw2].cons.append(tk)
                    if ic == 0:
                        for tt in range(4):
                            s_psJ[tt].cons = []
                def fbias(e, dc2=dc2):
                    for tt in range(4):
                        inst = e.matmul(psJ[tt][:], ones[0:1, 0:128],
                                        b2I[:, dc2 * 512:(dc2 + 1) * 512],
                                        start=False, stop=True)
                    return inst
                tkb = b.emit("t", fbias, waits=[s_bI.prod], inc="c")
                s_bI.cons.append(tkb)
                for tt in range(4):
                    s_psJ[tt].prod = tkb
                    s_psJ[tt].cons = []
                for tt in range(4):
                    c.xfer([s_psJ[tt]], s_o4[tt], "v",
                           lambda e, tt=tt, dc2=dc2: e.tensor_copy(
                               oI[:, tt * 1024 + dc2 * 512: tt * 1024 + (dc2 + 1) * 512],
                               psJ[tt][:]),
                           first=(dc2 == 0), last=(dc2 == 1))
            for tt in range(4):
                it = tt % 2
                lnI.run(c, s_o4[tt], s_lnI.ap, s_lnI)
                c.xfer([s_lnI, s_xI[i][tt]], s_yI[it], "v",
                       lambda e, it=it, i=i, tt=tt: e.tensor_add(
                           s_yI[it].ap, s_xI[i][tt].ap, lnoI[:]))
                c.consume([s_yI[it]], "s", lambda e, ts=ts, tt=tt, it=it: e.dma_start(
                    out=y[ts * 512 + tt * 128:ts * 512 + (tt + 1) * 128, :],
                    in_=s_yI[it].ap), inc="d")
        b.barrier()
        esI.close()

    return nc, gstack, dt



def _prep(inputs):
    f = np.float32
    pre = {}
    pre["wqkv"] = np.ascontiguousarray(inputs["L0_attn_w_qkv"], dtype=f)
    pre["wout"] = np.ascontiguousarray(inputs["L0_attn_w_out"], dtype=f)
    pre["r_w"] = np.ascontiguousarray(inputs["L0_ffn_router_w"], dtype=f)
    pre["r_b"] = np.ascontiguousarray(np.asarray(inputs["L0_ffn_router_b"]).reshape(1, 64), dtype=f)
    pre["w1"] = np.ascontiguousarray(inputs["L0_ffn_w1"], dtype=f)
    pre["b1T"] = np.ascontiguousarray(
        np.asarray(inputs["L0_ffn_b1"]).reshape(64, 2, 128).transpose(2, 1, 0).reshape(128, 128),
        dtype=f)
    pre["w2"] = np.ascontiguousarray(inputs["L0_ffn_w2"], dtype=f)
    pre["b2"] = np.ascontiguousarray(inputs["L0_ffn_b2"], dtype=f)
    pre["sw1"] = np.ascontiguousarray(inputs["L0_ffn_sw1"], dtype=f)
    pre["sb1T"] = np.ascontiguousarray(
        np.asarray(inputs["L0_ffn_sb1"]).reshape(2, 16, 128).transpose(2, 0, 1).reshape(128, 32),
        dtype=f)
    pre["sw2"] = np.ascontiguousarray(inputs["L0_ffn_sw2"], dtype=f)
    pre["sb2r"] = np.ascontiguousarray(np.asarray(inputs["L0_ffn_sb2"]).reshape(1, 2048), dtype=f)
    pre["wq1T"] = np.ascontiguousarray(np.asarray(inputs["L1_attn_Wq"]).T, dtype=f)
    pre["wk1T"] = np.ascontiguousarray(np.asarray(inputs["L1_attn_Wk"]).T, dtype=f)
    pre["wv1T"] = np.ascontiguousarray(np.asarray(inputs["L1_attn_Wv"]).T, dtype=f)
    pre["wo1T"] = np.ascontiguousarray(np.asarray(inputs["L1_attn_Wo"]).T, dtype=f)
    pre["ncst"] = np.ascontiguousarray(np.asarray(inputs["L1_attn_norm_const"]).reshape(1, 16), dtype=f)
    pre["w1f"] = np.ascontiguousarray(inputs["L1_ffn_W1"], dtype=f)
    pre["b1fT"] = np.ascontiguousarray(np.asarray(inputs["L1_ffn_b1"]).reshape(32, 128).T, dtype=f)
    pre["w2f"] = np.ascontiguousarray(inputs["L1_ffn_W2"], dtype=f)
    pre["b2f"] = np.ascontiguousarray(np.asarray(inputs["L1_ffn_b2"]).reshape(1, D), dtype=f)
    pre["eye"] = np.eye(128, dtype=f)
    return pre


def kernel(**inputs):
    pre = _prep(inputs)
    nc, gstack, _ = build()
    x = np.ascontiguousarray(np.asarray(inputs["L0_attn_x"]), dtype=np.float32)
    in_maps = []
    for b_i in range(NCORE):
        m = {"x_in": np.ascontiguousarray(x[b_i])}
        m.update(pre)
        in_maps.append(m)
    res = run_bass_kernel_spmd(nc, in_maps, core_ids=list(range(NCORE)))
    gstack.close()
    return np.stack([r["y"] for r in res.results], axis=0)
